# version 03 (final)
# baseline (speedup 1.0000x reference)
import contextlib
import numpy as np
import ml_dtypes
import concourse.bass as bass
import concourse.mybir as mybir
from concourse.bass_utils import run_bass_kernel_spmd

F32 = mybir.dt.float32
BF16 = mybir.dt.bfloat16
ALU = mybir.AluOpType
AF = mybir.ActivationFunctionType
AX = mybir.AxisListType.X

NCORE = 8
D = 1024
KD = 8
CH = 512
NHT = 32
NE = 8
H_MOE = 4096
EPS = 1e-5


class Sem:

    def __init__(self, ctx, nc, name):
        self.h = ctx.enter_context(nc.semaphore(name))
        self.n = 0

    def inc(self, instr, k=1):
        instr.then_inc(self.h, k)
        self.n += k

    def w(self, eng, val=None):
        v = self.n if val is None else val
        assert v >= 0, f"negative wait on {self.h}"
        eng.wait_ge(self.h, v)



_DYN_N = [0]


class DynCalc:

    def __init__(self, eng, ctx):
        self.eng = eng
        names = []
        for _ in range(4):
            _DYN_N[0] += 1
            names.append(f"dyn{_DYN_N[0]}")
        self.rr = [ctx.enter_context(eng.register(names[i])) for i in range(3)]
        self.rt = ctx.enter_context(eng.register(names[3]))
        self.i = 0

    def val(self, cst, *terms):
        eng = self.eng
        r = self.rr[self.i]
        self.i = (self.i + 1) % 3
        (v0, c0) = terms[0]
        eng.reg_alu(r, v0, c0, ALU.mult)
        for (v, c) in terms[1:]:
            eng.reg_alu(self.rt, v, c, ALU.mult)
            eng.reg_alu(r, r, self.rt, ALU.add)
        if cst:
            eng.reg_alu(r, r, int(cst), ALU.add)
        return r

    def wait(self, sem, cst, *terms):
        assert cst >= 0, cst
        self.eng.wait_ge(sem.h, self.val(cst, *terms))

    def ap(self, template_ap, cst, *terms):
        off = self.val(int(cst) + int(template_ap.offset), *terms)
        return bass.AP(template_ap.tensor, off, list(template_ap.ap))



def emit_moe_sublayer(nc, ctx, block, T, io, tag=""):
    NCH = T // CH
    TH = T // 2
    NSUB = TH // CH
    NTK = T // 128

    sb = lambda name, shape, dt: ctx.enter_context(nc.sbuf_tensor(name + tag, shape, dt))
    xn = sb("xn", [128, KD, T], BF16)
    xr = sb("xr", [128, KD, CH], F32)
    xsq = sb("xsq", [128, 1, CH], F32)
    h_sb = sb("h_sb", [128, NHT, TH], BF16)
    oacc = sb("oacc", [128, KD, TH], F32)
    w1r = sb("w1r", [128, KD, 2, 2, 128], BF16)
    w2r = sb("w2r", [128, NHT, 2, 128], BF16)
    gbe = sb("gbe", [128, NSUB, CH], F32)
    grow = sb("grow", [1, TH], F32)
    rw_sb = sb("rw_sb", [128, KD, 8], BF16)
    rb_sb = sb("rb_sb", [8, 1], F32)
    b1_sb = sb("b1_sb", [128, NE, NHT], F32)
    b1cur = sb("b1cur", [128, NHT], F32)
    b2cur = sb("b2cur", [128, KD], F32)
    b2_sb = sb("b2_sb", [128, NE, KD], F32)
    idf = sb("idf", [128, 128], F32)
    onec = sb("onec", [128, 1], F32)
    oner = sb("oner", [1, 128], F32)
    mrow = sb("mrow", [1, CH], F32)
    msrow = sb("msrow", [1, CH], F32)
    vrow = sb("vrow", [1, CH], F32)
    lnv = msrow
    rstd = sb("rstd", [1, CH], F32)
    acoef = sb("acoef", [1, CH], F32)
    tmpv = sb("tmpv", [128, CH], F32)
    tmpB = tmpv
    lT = sb("lT", [8, T], F32)
    gT = lT
    ltok = sb("ltok", [128, 8], F32)
    gtok = sb("gtok", [128, 8], F32)
    m1 = sb("m1", [128, 1], F32)
    negm1 = sb("negm1", [128, 1], F32)
    l2t = sb("l2t", [128, 8], F32)
    m2 = sb("m2", [128, 1], F32)
    keep = sb("keep", [128, 8], F32)
    exr = sb("exr", [128, 8], F32)
    ssum = sb("ssum", [128, 1], F32)
    rsum = sb("rsum", [128, 1], F32)

    ps = ctx.enter_context(nc.psum_tensor("ps" + tag, [128, 8, 512], F32))

    S = lambda name: Sem(ctx, nc, name + tag)
    s_cst, s_gcst, s_ones = S("s_cst"), S("s_gcst"), S("s_ones")
    s_x, s_xfree, s_xsqv, s_sqmm = S("s_x"), S("s_xfree"), S("s_xsqv"), S("s_sqmm")
    s_vst, s_sst, s_rbab = S("s_vst"), S("s_sst"), S("s_rbab")
    s_lgmm, s_lg, s_ltr, s_vr, s_sex = S("s_lgmm"), S("s_lg"), S("s_ltr"), S("s_vr"), S("s_sex")
    s_gtr, s_gev, s_gd, s_grow = S("s_gtr"), S("s_gev"), S("s_gd"), S("s_grow")
    s_w1 = [S("s_w1a"), S("s_w1b")]
    s_w2 = [S("s_w2a"), S("s_w2b")]
    s_mmA, s_relu = S("s_mmA"), S("s_relu")
    s_gbmm, s_gbev, s_ymm, s_vB = S("s_gbmm"), S("s_gbev"), S("s_ymm"), S("s_vB")
    s_xres, s_res, s_out = S("s_xres"), S("s_res"), S("s_out")
    s_b1c = S("s_b1c")

    x3 = io["xT_in"].rearrange("p (t j) -> p t j", t=KD)
    xo3 = io["xT_out"].rearrange("p (t j) -> p t j", t=KD)
    w1h = io["w1"].rearrange("e (t p) h -> e p t h", p=128)
    w2h = io["w2"].rearrange("e (t p) d -> e p t d", p=128)

    @block.sync
    def _(sy):
        sy.dma_start(rb_sb[:, :], io["rb"][:, :]).then_inc(s_cst.h, 16)
        sy.dma_start(b1_sb[:, :, :], io["b1r"][:, :, :]).then_inc(s_cst.h, 16)
        sy.dma_start(b2_sb[:, :, :], io["b2r"][:, :, :]).then_inc(s_cst.h, 16)
        sy.dma_start(idf[:, :], io["identf"][:, :]).then_inc(s_cst.h, 16)
        s_cst.n += 64

    @block.gpsimd
    def _(g):
        g.dma_start(rw_sb[:, :, :], io["rw"].rearrange("(t p) e -> p t e", p=128)).then_inc(s_gcst.h, 16)
        s_gcst.n += 16
        g.memset(onec[:, :], 1.0).then_inc(s_ones.h, 1)
        g.memset(oner[:, :], 1.0).then_inc(s_ones.h, 1)
        s_ones.n += 2

    @block.scalar
    def _(s):
        for _i in range(8):
            s.nop().then_inc(s_relu.h, 1)
        s_relu.n += 8

    @block.vector
    def _(v):
        for _i in range(8):
            v.nop().then_inc(s_vB.h, 1)
        s_vB.n += 8

    for c in range(NCH):
        csl = slice(c * CH, (c + 1) * CH)

        @block.sync
        def _(sy, c=c):
            if c > 0:
                sy.wait_ge(s_vst.h, 22 * c)
            sy.dma_start(xr[:, :, :], x3[:, :, slice(c * CH, (c + 1) * CH)]).then_inc(s_x.h, 16)
            s_x.n += 16

        @block.vector
        def _(v, c=c):
            s_x.w(v)
            for t in range(KD):
                m = c * KD + t
                if m >= 1:
                    s_sqmm.w(v, m)
                v.tensor_mul(xsq[:, 0, :], xr[:, t, :], xr[:, t, :]).then_inc(s_xsqv.h, 1)
                s_xsqv.n += 1

        @block.tensor
        def _(t_, c=c):
            s_gcst.w(t_, 16)
            s_ones.w(t_, 2)
            s_x.w(t_)
            if c > 0:
                s_vst.w(t_, 22 * (c - 1) + 2)
            for t in range(KD):
                t_.matmul(ps[0:1, 7, :], onec[:, :], xr[:, t, :],
                          start=(t == 0), stop=(t == KD - 1))
            for t in range(KD):
                s_xsqv.w(t_, c * KD + t + 1)
                mm = t_.matmul(ps[0:1, 6, :], onec[:, :], xsq[:, 0, :],
                               start=(t == 0), stop=(t == KD - 1))
                s_sqmm.inc(mm)

        @block.vector
        def _(v, c=c):
            s_sqmm.w(v)
            v.tensor_scalar(mrow[:, :], ps[0:1, 7, :], 1.0 / D, None, ALU.mult).then_inc(s_vst.h, 1)
            v.tensor_scalar(msrow[:, :], ps[0:1, 6, :], 1.0 / D, None, ALU.mult).then_inc(s_vst.h, 1)
            s_vst.n += 2
            s_vst.w(v)
            v.scalar_tensor_tensor(vrow[:, :], mrow[:, :], -1.0, mrow[:, :], ALU.mult, ALU.mult).then_inc(s_vst.h, 1)
            s_vst.n += 1
            s_vst.w(v)
            v.tensor_add(vrow[:, :], vrow[:, :], msrow[:, :]).then_inc(s_vst.h, 1)
            s_vst.n += 1
            s_vst.w(v)
            v.tensor_scalar(vrow[:, :], vrow[:, :], EPS, None, ALU.add).then_inc(s_vst.h, 1)
            s_vst.n += 1

        @block.scalar
        def _(s, c=c):
            s_vst.w(s)
            s.activation(lnv[:, :], vrow[:, :], AF.Ln).then_inc(s_sst.h, 1)
            s_sst.n += 1
            s_sst.w(s)
            s.activation(rstd[:, :], lnv[:, :], AF.Exp, scale=-0.5).then_inc(s_sst.h, 1)
            s_sst.n += 1

        @block.vector
        def _(v, c=c):
            s_sst.w(v)
            v.scalar_tensor_tensor(acoef[:, :], mrow[:, :], -1.0, rstd[:, :], ALU.mult, ALU.mult).then_inc(s_vst.h, 1)
            s_vst.n += 1

        @block.tensor
        def _(t_, c=c):
            s_vst.w(t_)
            t_.matmul(ps[:, 4, :], oner[0:1, :], rstd[:, :], start=True, stop=True).then_inc(s_rbab.h, 1)
            t_.matmul(ps[:, 5, :], oner[0:1, :], acoef[:, :], start=True, stop=True).then_inc(s_rbab.h, 1)
            s_rbab.n += 2

        @block.vector
        def _(v, c=c):
            s_rbab.w(v)
            for t in range(KD):
                s_vst.w(v)
                mul = v.tensor_mul(tmpv[:, :], xr[:, t, :], ps[:, 4, :])
                mul.then_inc(s_vst.h, 1)
                s_vst.n += 1
                s_vst.w(v)
                add = v.tensor_add(xn[:, t, slice(c * CH, (c + 1) * CH)], tmpv[:, :], ps[:, 5, :])
                add.then_inc(s_vst.h, 1)
                s_vst.n += 1

    for c in range(NCH):
        @block.tensor
        def _(t_, c=c):
            s_vst.w(t_, 22 * (c + 1))
            if c > 0:
                s_lg.w(t_, c)
            mm = None
            for t in range(KD):
                mm = t_.matmul(ps[0:8, 0, :], rw_sb[:, t, :], xn[:, t, slice(c * CH, (c + 1) * CH)],
                               start=(t == 0), stop=(t == KD - 1))
            s_lgmm.inc(mm)

        @block.vector
        def _(v, c=c):
            s_lgmm.w(v)
            s_cst.w(v, 64)
            v.tensor_scalar(lT[:, slice(c * CH, (c + 1) * CH)], ps[0:8, 0, :],
                            rb_sb[0:8, 0:1], None, ALU.add).then_inc(s_lg.h, 1)
            s_lg.n += 1

    for k in range(NTK):
        ksl = slice(k * 128, (k + 1) * 128)

        @block.tensor
        def _(t_, k=k):
            s_lg.w(t_, (k * 128) // CH + 1)
            s_cst.w(t_, 64)
            if k > 0:
                s_vr.w(t_, 10 * (k - 1) + 1)
            mm = t_.transpose(ps[:, 1, 0:8], lT[0:8, slice(k * 128, (k + 1) * 128)], idf[0:8, 0:8])
            s_ltr.inc(mm)

        @block.vector
        def _(v, k=k):
            s_ltr.w(v)
            if k > 0:
                s_sex.w(v, k)
            v.tensor_copy(ltok[:, :], ps[:, 1, 0:8]).then_inc(s_vr.h, 1)
            s_vr.n += 1
            s_vr.w(v)
            v.reduce_max(m1[:, :], ltok[:, :], axis=AX).then_inc(s_vr.h, 1)
            s_vr.n += 1
            s_vr.w(v)
            v.tensor_scalar(l2t[:, :], ltok[:, :], m1[:, 0:1], -1e30, ALU.is_ge, ALU.mult).then_inc(s_vr.h, 1)
            s_vr.n += 1
            s_vr.w(v)
            v.tensor_add(l2t[:, :], l2t[:, :], ltok[:, :]).then_inc(s_vr.h, 1)
            s_vr.n += 1
            s_vr.w(v)
            v.reduce_max(m2[:, :], l2t[:, :], axis=AX).then_inc(s_vr.h, 1)
            s_vr.n += 1
            s_vr.w(v)
            v.tensor_scalar(keep[:, :], ltok[:, :], m2[:, 0:1], None, ALU.is_ge).then_inc(s_vr.h, 1)
            s_vr.n += 1
            s_vr.w(v)
            v.tensor_scalar(negm1[:, :], m1[:, 0:1], -1.0, None, ALU.mult).then_inc(s_vr.h, 1)
            s_vr.n += 1

        @block.scalar
        def _(s, k=k):
            s_vr.w(s)
            s.activation(exr[:, :], ltok[:, :], AF.Exp, bias=negm1[:, 0:1], scale=1.0).then_inc(s_sex.h, 1)
            s_sex.n += 1

        @block.vector
        def _(v, k=k):
            s_sex.w(v)
            v.tensor_mul(exr[:, :], exr[:, :], keep[:, :]).then_inc(s_vr.h, 1)
            s_vr.n += 1
            s_vr.w(v)
            v.reduce_sum(ssum[:, :], exr[:, :], axis=AX).then_inc(s_vr.h, 1)
            s_vr.n += 1
            s_vr.w(v)
            v.reciprocal(rsum[:, :], ssum[:, :]).then_inc(s_vr.h, 1)
            s_vr.n += 1
            s_vr.w(v)
            v.tensor_scalar(gtok[:, :], exr[:, :], rsum[:, 0:1], None, ALU.mult).then_inc(s_vr.h, 1)
            s_vr.n += 1

        @block.tensor
        def _(t_, k=k):
            s_vr.w(t_)
            if k > 0:
                s_gev.w(t_, k)
            mm = t_.transpose(ps[0:8, 2, 0:128], gtok[:, :], idf[:, :])
            s_gtr.inc(mm)

        @block.scalar
        def _(s, k=k):
            s_gtr.w(s)
            s.copy(gT[:, slice(k * 128, (k + 1) * 128)], ps[0:8, 2, 0:128]).then_inc(s_gev.h, 1)
            s_gev.n += 1

    @block.gpsimd
    def _(g):
        s_gev.w(g)
        g.dma_start(io["g_dram"][:, :], gT[:, :]).then_inc(s_gd.h, 16)
        s_gd.n += 16


    dynt = DynCalc(nc.tensor, ctx)
    dyns = DynCalc(nc.scalar, ctx)
    dynv = DynCalc(nc.vector, ctx)

    Bgrow0, Bw10, Bw20 = s_grow.n, s_w1[0].n + s_w1[1].n, s_w2[0].n + s_w2[1].n
    Bw1p = [s_w1[0].n, s_w1[1].n]
    Bw2p = [s_w2[0].n, s_w2[1].n]
    BmmA0, Brelu0, Bgbmm0, Bgbev0 = s_mmA.n, s_relu.n, s_gbmm.n, s_gbev.n
    Bymm0, BvB0 = s_ymm.n, s_vB.n
    Bres0, Bxres0, Bout0 = s_res.n, s_xres.n, s_out.n
    D_grow, D_w1, D_w2 = 16 * NE, 256 * NE, 128 * NE
    D_mmA, D_relu, D_gbmm, D_gbev = 32 * NE, 64 * NE, 2 * NE, 2 * NE
    D_ymm, D_vB = 16 * NE, 1 + 32 * NE
    D_res, D_xres, D_out = 8 * NSUB, 16 * NSUB, 16 * NSUB

    @block.gpsimd
    def _(g):
        def w2_load(e, wi):
            di = (s_w2[0].n + s_w2[1].n - Bw20) // 16
            if di >= 2:
                g.wait_ge(s_ymm.h, Bymm0 + 2 * (di - 1))
            g.dma_start(w2r[:, :, wi % 2, :],
                        w2h[e, :, :, wi * 128:(wi + 1) * 128]).then_inc(s_w2[wi % 2].h, 16)
            s_w2[wi % 2].n += 16

        g.wait_ge(s_gd.h, s_gd.n)
        prev = None
        for th in range(2):
            jb = th * TH
            for e in range(NE):
                if prev is not None:
                    for wi in range(2, 8):
                        w2_load(prev, wi)
                g.wait_ge(s_gbmm.h, Bgbmm0 + D_gbmm * th + 2 * e)
                g.dma_start(grow[:, :], io["g_dram"][e:e + 1, jb:jb + TH]).then_inc(s_grow.h, 16)
                s_grow.n += 16
                for i in range(16):
                    gi = (s_w1[0].n + s_w1[1].n - Bw10) // 16
                    if gi >= 2:
                        g.wait_ge(s_mmA.h, BmmA0 + 2 * (gi - 1))
                    g.dma_start(
                        w1r[:, :, i % 2, :, :],
                        w1h[e, :, :, i * 256:(i + 1) * 256].rearrange("p t (u n) -> p t u n", u=2),
                    ).then_inc(s_w1[i % 2].h, 16)
                    s_w1[i % 2].n += 16
                for wi in range(2):
                    w2_load(e, wi)
                prev = e
        for wi in range(2, 8):
            w2_load(prev, wi)

    for th in range(2):
        jbase = th * TH
        first = th == 0
        Bgrow = Bgrow0 + D_grow * th
        Bw1p_th = [Bw1p[0] + 128 * NE * th, Bw1p[1] + 128 * NE * th]
        Bw2p_th = [Bw2p[0] + 64 * NE * th, Bw2p[1] + 64 * NE * th]
        Brelu = Brelu0 + D_relu * th
        Bgbev = Bgbev0 + D_gbev * th
        Bgbmm = Bgbmm0 + D_gbmm * th
        Bymm = Bymm0 + D_ymm * th
        BvB1 = BvB0 + D_vB * th + 1

        @block.tensor
        def _(t_, th=th, Bgrow=Bgrow, Bw1p_th=Bw1p_th, Bw2p_th=Bw2p_th, Brelu=Brelu,
              Bgbev=Bgbev, Bymm=Bymm, BvB1=BvB1, jbase=jbase, first=first):
            if first:
                s_lg.w(t_)
                s_vr.w(t_)
                s_gev.w(t_)
                s_vst.w(t_)
            with t_.Fori(0, NE) as ee:
                dynt.wait(s_grow, Bgrow + 16, (ee, 16))
                dynt.wait(s_gbev, Bgbev, (ee, 2))
                for u in range(NSUB):
                    mm = t_.matmul(ps[:, 6 + u, :], oner[0:1, :], grow[0:1, u * CH:(u + 1) * CH],
                                   start=True, stop=True)
                    mm.then_inc(s_gbmm.h, 1)
                dynt.wait(s_gbev, Bgbev + 2, (ee, 2))
                dynt.wait(s_ymm, Bymm, (ee, 16))
                dynt.wait(s_vB, BvB1, (ee, 32))
                with t_.Fori(0, 8) as gq:
                    for par in range(2):
                        dynt.wait(s_w1[par], Bw1p_th[par] + 16, (ee, 128), (gq, 16))
                        for u in range(2):
                            dynt.wait(s_relu, Brelu + 4 * par + 2 * u - 2, (ee, 64), (gq, 8))
                            for sub in range(NSUB):
                                bank = 2 * (u % 2) + sub
                                mm = None
                                for t in range(KD):
                                    mm = t_.matmul(ps[:, bank, :],
                                                   w1r[:, t, par, u, :],
                                                   xn[:, t, jbase + sub * CH:jbase + (sub + 1) * CH],
                                                   start=(t == 0), stop=(t == KD - 1))
                            mm.then_inc(s_mmA.h, 1)
                dynt.wait(s_relu, Brelu + 64, (ee, 64))
                with t_.Fori(0, 4) as dd:
                    dynt.wait(s_vB, BvB1, (ee, 32), (dd, 8))
                    for half in range(2):
                        dynt.wait(s_w2[half], Bw2p_th[half] + 16, (ee, 64), (dd, 16))
                        for sub in range(NSUB):
                            bank = 2 + 2 * half + sub
                            mm = None
                            for kt in range(NHT):
                                mm = t_.matmul(ps[:, bank, :],
                                               w2r[:, kt, half, :],
                                               h_sb[:, kt, sub * CH:(sub + 1) * CH],
                                               start=(kt == 0), stop=(kt == NHT - 1))
                            mm.then_inc(s_ymm.h, 1)
        s_gbmm.n += NE * 2
        s_mmA.n += NE * 32
        s_ymm.n += NE * 16

    @block.scalar
    def _(s):
        with s.Fori(0, 2) as tt:
            with s.Fori(0, NE) as ee:
                dyns.wait(s_ymm, Bymm0, (tt, D_ymm), (ee, 16))
                dyns.wait(s_b1c, 1, (tt, 2 * NE), (ee, 2))
                for ht in range(NHT):
                    dyns.wait(s_mmA, BmmA0 + ht + 1, (tt, D_mmA), (ee, 32))
                    for sub in range(NSUB):
                        bank = 2 * (ht % 2) + sub
                        a = s.activation(h_sb[:, ht, sub * CH:(sub + 1) * CH],
                                         ps[:, bank, :], AF.Relu, bias=b1cur[:, ht:ht + 1])
                        a.then_inc(s_relu.h, 1)
    s_relu.n += 2 * NE * 64

    @block.vector
    def _(v):
        with v.Fori(0, 2) as tt:
            dynv.wait(s_res, Bres0, (tt, D_res))
            v.memset(oacc[:, :, :], 0.0).then_inc(s_vB.h, 1)
            with v.Fori(0, NE) as ee:
                dynv.wait(s_relu, Brelu0, (tt, D_relu), (ee, 64))
                bsrc = dynv.ap(b1_sb[:, 0, :], 0, (ee, NHT))
                v.tensor_copy(b1cur[:, :], bsrc).then_inc(s_b1c.h, 1)
                dynv.wait(s_gbmm, Bgbmm0 + 2, (tt, D_gbmm), (ee, 2))
                dynv.wait(s_vB, BvB0 + 1, (tt, D_vB), (ee, 32))
                b2src = dynv.ap(b2_sb[:, 0, :], 0, (ee, KD))
                v.tensor_copy(b2cur[:, :], b2src).then_inc(s_b1c.h, 1)
                for u in range(NSUB):
                    v.tensor_copy(gbe[:, u, :], ps[:, 6 + u, :]).then_inc(s_gbev.h, 1)
                for dd in range(4):
                    for half in range(2):
                        d = 2 * dd + half
                        for sub in range(NSUB):
                            bank = 2 + 2 * half + sub
                            idx = 8 * dd + 4 * half + 2 * sub
                            dynv.wait(s_ymm, Bymm0 + 4 * dd + 2 * half + sub + 1, (tt, D_ymm), (ee, 16))
                            dynv.wait(s_vB, BvB0 + 1 + idx, (tt, D_vB), (ee, 32))
                            bia = b2cur[:, d:d + 1]
                            st = v.scalar_tensor_tensor(tmpB[:, :], ps[:, bank, :],
                                                        bia, gbe[:, sub, :], ALU.add, ALU.mult)
                            st.then_inc(s_vB.h, 1)
                            dynv.wait(s_vB, BvB0 + 2 + idx, (tt, D_vB), (ee, 32))
                            dst = oacc[:, d, sub * CH:(sub + 1) * CH]
                            cp = v.tensor_tensor(dst, dst, tmpB[:, :], ALU.add)
                            cp.then_inc(s_vB.h, 1)
            for u in range(NSUB):
                dynv.wait(s_xres, Bxres0 + 16 * (u + 1), (tt, D_xres))
                dynv.wait(s_vB, BvB0 + D_vB, (tt, D_vB))
                for t in range(KD):
                    v.scalar_tensor_tensor(xr[:, t, :], oacc[:, t, u * CH:(u + 1) * CH], 0.5,
                                           xr[:, t, :], ALU.mult, ALU.add).then_inc(s_res.h, 1)
    s_vB.n += 2 * (1 + NE * 32)
    s_b1c.n += 2 * 2 * NE
    s_gbev.n += 2 * NE * 2
    s_res.n += 2 * NSUB * KD

    @block.sync
    def _(sy):
        for th in range(2):
            for u in range(NSUB):
                if th == 0 and u == 0:
                    sy.wait_ge(s_vst.h, 22 * NCH)
                else:
                    sy.wait_ge(s_out.h, Bout0 + 16 * (NSUB * th + u - 1) + 16)
                sy.dma_start(xr[:, :, :], x3[:, :, slice(th * TH + u * CH, th * TH + (u + 1) * CH)]).then_inc(s_xres.h, 16)
                s_xres.n += 16
                sy.wait_ge(s_res.h, Bres0 + D_res * th + 8 * (u + 1))
                sy.dma_start(xo3[:, :, slice(th * TH + u * CH, th * TH + (u + 1) * CH)], xr[:, :, :]).then_inc(s_out.h, 16)
                s_out.n += 16

    @block.sync
    def _(sy):
        s_out.w(sy)

    return {"s_out": s_out}







def emit_l0attn_sublayer(nc, ctx, block, T, io, tag="a0"):
    NCH = T // CH
    NS = 2 * T // CH
    NJT = T // 128
    SCALE = 1.0 / 32.0

    sb = lambda name, shape, dt: ctx.enter_context(nc.sbuf_tensor(name + tag, shape, dt))
    qT = sb("qT", [128, KD, T], BF16)
    vT = sb("vT", [128, KD, T], BF16)
    wbuf = sb("wbuf", [128, KD, D], BF16)
    xr = sb("xr", [128, KD, CH], F32)
    mrow = sb("mrow", [1, CH], F32)
    msrow = sb("msrow", [1, CH], F32)
    lnv = msrow
    vrow = sb("vrow", [1, CH], F32)
    rstd = sb("rstd", [1, CH], F32)
    acoef = sb("acoef", [1, CH], F32)
    tmpv = sb("tmpv", [128, CH], F32)
    ztmp = sb("ztmp", [128, CH], F32)
    idb = sb("idb", [128, 128], BF16)
    onec = sb("onec", [128, 1], F32)
    oner = sb("oner", [1, 128], F32)
    gam = sb("gam", [128, KD], F32)
    bet = sb("bet", [128, KD], F32)
    bfc = sb("bfc", [128, KD], F32)
    qm = sb("qm", [1, 1], F32)
    km = sb("km", [1, 1], F32)
    kmg = sb("kmg", [1, 1], F32)
    nB = sb("nB", [128, 1], F32)
    srow = tmpv
    rrow = vrow

    ps = ctx.enter_context(nc.psum_tensor("ps" + tag, [128, 7, 512], F32))

    S = lambda name: Sem(ctx, nc, name + tag)
    s_cst, s_wld, s_ones, s_idb = S("s_cst"), S("s_wld"), S("s_ones"), S("s_idb")
    s_x, s_xsqv, s_sqmm = S("s_x"), S("s_xsqv"), S("s_sqmm")
    s_vst, s_sst, s_rbab = S("s_vst"), S("s_sst"), S("s_rbab")
    s_pmm, s_pev = S("s_pmm"), S("s_pev")
    s_tp, s_tpe = S("s_tp"), S("s_tpe")
    s_bsq, s_bmm, s_bmax = S("s_bsq"), S("s_bmm"), S("s_bmax")
    s_ccs, s_cck, s_ccv, s_ccm, s_kml = S("s_ccs"), S("s_cck"), S("s_ccv"), S("s_ccm"), S("s_kml")
    s_kld = [S("s_klda"), S("s_kldb")]
    s_vld = [S("s_vlda"), S("s_vldb")]
    s_mld = [S("s_mlda"), S("s_mldb")]
    s_smm, s_madd, s_exp = S("s_smm"), S("s_madd"), S("s_exp")
    s_avmm, s_avac, s_sumc = S("s_avmm"), S("s_avac"), S("s_sumc")
    s_ep, s_bc, s_st2, s_fcmm = S("s_ep"), S("s_bc"), S("s_st2"), S("s_fcmm")
    s_zsq, s_zmm = S("s_zsq"), S("s_zmm")
    s_xres, s_res, s_out = S("s_xres"), S("s_res"), S("s_out")

    x3 = io["xT_in"].rearrange("p (t j) -> p t j", t=KD)
    xo3 = io["xT_out"].rearrange("p (t j) -> p t j", t=KD)
    cko = io["cck_out"].rearrange("(r p) (t j) -> r p t j", p=128, t=KD)
    cvo = io["ccv_out"].rearrange("(r jt p) v -> r p jt v", p=128, jt=NJT)
    msk4 = io["maskstream"].rearrange("s p (jt q) -> s p jt q", jt=4)

    @block.sync
    def _(sy):
        sy.dma_start(gam[:, :], io["gamr"][:, :]).then_inc(s_cst.h, 16)
        sy.dma_start(bet[:, :], io["betr"][:, :]).then_inc(s_cst.h, 16)
        sy.dma_start(bfc[:, :], io["bfcr"][:, :]).then_inc(s_cst.h, 16)
        s_cst.n += 48

    @block.gpsimd
    def _(g):
        g.dma_start(idb[:, :], io["identf"][:, :]).then_inc(s_idb.h, 16)
        s_idb.n += 16
        g.memset(onec[:, :], 1.0).then_inc(s_ones.h, 1)
        g.memset(oner[:, :], 1.0).then_inc(s_ones.h, 1)
        s_ones.n += 2
        for wname, thr in [("Wq", 0), ("Wk", 32), ("Wv", 64),
                           ("Wk", 128), ("Wv", 160), ("Wfc", 192)]:
            if thr:
                g.wait_ge(s_pmm.h, thr)
            g.dma_start(wbuf[:, :, :], io[wname].rearrange("(t p) o -> p t o", p=128)).then_inc(s_wld.h, 16)
            s_wld.n += 16

    with contextlib.ExitStack() as pctx:
        xn = pctx.enter_context(nc.sbuf_tensor("xn" + tag, [128, KD, T], BF16))
        kTt = pctx.enter_context(nc.sbuf_tensor("kTt" + tag, [128, KD, CH], BF16))
        vtm = pctx.enter_context(nc.sbuf_tensor("vtm" + tag, [128, NJT, 1026], BF16))
        xsq = pctx.enter_context(nc.sbuf_tensor("xsq" + tag, [128, CH], F32))
        qsq = xsq

        xoth3 = io["xT_other"].rearrange("p (t j) -> p t j", t=KD)
        pg = [0]
        grp = [0]
        bsqc = [0]

        def emit_ln_chunks(half):
            x3h = x3 if half == 0 else xoth3
            for c in range(NCH):
                cg = half * NCH + c

                @block.sync
                def _(sy, c=c, cg=cg, x3h=x3h):
                    if cg > 0:
                        sy.wait_ge(s_vst.h, 22 * cg)
                    sy.dma_start(xr[:, :, :], x3h[:, :, c * CH:(c + 1) * CH]).then_inc(s_x.h, 16)
                    s_x.n += 16

                @block.vector
                def _(v, c=c, cg=cg):
                    s_x.w(v)
                    if cg == NCH:
                        s_bmm.w(v)
                    for t in range(KD):
                        m = cg * KD + t
                        if m >= 1:
                            s_sqmm.w(v, m)
                        v.tensor_mul(xsq[:, :], xr[:, t, :], xr[:, t, :]).then_inc(s_xsqv.h, 1)
                        s_xsqv.n += 1

                @block.tensor
                def _(t_, c=c, cg=cg):
                    s_ones.w(t_, 2)
                    s_x.w(t_)
                    if cg > 0:
                        s_vst.w(t_, 22 * (cg - 1) + 2)
                    if cg == NCH:
                        s_bmax.w(t_)
                    for t in range(KD):
                        t_.matmul(ps[0:1, 6, :], onec[:, :], xr[:, t, :],
                                  start=(t == 0), stop=(t == KD - 1))
                    for t in range(KD):
                        s_xsqv.w(t_, cg * KD + t + 1)
                        mm = t_.matmul(ps[0:1, 5, :], onec[:, :], xsq[:, :],
                                       start=(t == 0), stop=(t == KD - 1))
                        s_sqmm.inc(mm)

                @block.vector
                def _(v, c=c):
                    s_sqmm.w(v)
                    v.tensor_scalar(mrow[:, :], ps[0:1, 6, :], 1.0 / D, None, ALU.mult).then_inc(s_vst.h, 1)
                    v.tensor_scalar(msrow[:, :], ps[0:1, 5, :], 1.0 / D, None, ALU.mult).then_inc(s_vst.h, 1)
                    s_vst.n += 2
                    s_vst.w(v)
                    v.scalar_tensor_tensor(vrow[:, :], mrow[:, :], -1.0, mrow[:, :], ALU.mult, ALU.mult).then_inc(s_vst.h, 1)
                    s_vst.n += 1
                    s_vst.w(v)
                    v.tensor_add(vrow[:, :], vrow[:, :], msrow[:, :]).then_inc(s_vst.h, 1)
                    s_vst.n += 1
                    s_vst.w(v)
                    v.tensor_scalar(vrow[:, :], vrow[:, :], EPS, None, ALU.add).then_inc(s_vst.h, 1)
                    s_vst.n += 1

                @block.scalar
                def _(s, c=c):
                    s_vst.w(s)
                    s.activation(lnv[:, :], vrow[:, :], AF.Ln).then_inc(s_sst.h, 1)
                    s_sst.n += 1
                    s_sst.w(s)
                    s.activation(rstd[:, :], lnv[:, :], AF.Exp, scale=-0.5).then_inc(s_sst.h, 1)
                    s_sst.n += 1

                @block.vector
                def _(v, c=c):
                    s_sst.w(v)
                    v.scalar_tensor_tensor(acoef[:, :], mrow[:, :], -1.0, rstd[:, :], ALU.mult, ALU.mult).then_inc(s_vst.h, 1)
                    s_vst.n += 1

                @block.tensor
                def _(t_, c=c):
                    s_vst.w(t_)
                    t_.matmul(ps[:, 3, :], oner[0:1, :], rstd[:, :], start=True, stop=True).then_inc(s_rbab.h, 1)
                    t_.matmul(ps[:, 4, :], oner[0:1, :], acoef[:, :], start=True, stop=True).then_inc(s_rbab.h, 1)
                    s_rbab.n += 2

                @block.vector
                def _(v, c=c, cg=cg):
                    s_rbab.w(v)
                    if cg == NCH:
                        s_pmm.w(v)
                    for t in range(KD):
                        s_vst.w(v)
                        v.tensor_mul(tmpv[:, :], xr[:, t, :], ps[:, 3, :]).then_inc(s_vst.h, 1)
                        s_vst.n += 1
                        s_vst.w(v)
                        v.tensor_add(xn[:, t, c * CH:(c + 1) * CH], tmpv[:, :], ps[:, 4, :]).then_inc(s_vst.h, 1)
                        s_vst.n += 1

        def emit_kv(half):
            wld_k = 32 if half == 0 else 64
            wld_v = 48 if half == 0 else 80
            vst_full = s_vst.n

            for c in range(NCH):
                gbase = pg[0]
                pg[0] += KD

                @block.tensor
                def _(t_, c=c, half=half, gbase=gbase, vst_full=vst_full):
                    if c == 0:
                        t_.wait_ge(s_wld.h, wld_k)
                        if half == 1:
                            s_vst.w(t_, vst_full)
                    for o in range(KD):
                        gidx = gbase + o
                        s_pev.w(t_, gidx - 1)
                        bank = gidx % 2
                        mm = None
                        for t in range(KD):
                            mm = t_.matmul(ps[:, bank, :], wbuf[:, t, o * 128:(o + 1) * 128],
                                           xn[:, t, c * CH:(c + 1) * CH],
                                           start=(t == 0), stop=(t == KD - 1))
                        s_pmm.inc(mm)

                first_k = (half == 0 and c == 0)
                ccs_pre = s_ccs.n
                bsq_pre = s_bsq.n

                @block.scalar
                def _(s, c=c, gbase=gbase, first_k=first_k, ccs_pre=ccs_pre, bsq_pre=bsq_pre):
                    if not first_k:
                        s.wait_ge(s_ccs.h, ccs_pre)
                        s.wait_ge(s_bsq.h, bsq_pre)
                    for o in range(KD):
                        gidx = gbase + o
                        s_pmm.w(s, gidx + 1)
                        s.copy(kTt[:, o, :], ps[:, gidx % 2, :]).then_inc(s_pev.h, 1)
                        s_pev.n += 1

                @block.sync
                def _(sy, c=c, half=half):
                    sy.wait_ge(s_pev.h, s_pev.n)
                    sy.dma_start(cko[half, :, :, c * CH:(c + 1) * CH],
                                 kTt[:, :, :]).then_inc(s_ccs.h, 16)
                    s_ccs.n += 16

                @block.vector
                def _(v, c=c):
                    s_pev.w(v)
                    for t in range(KD):
                        m = bsqc[0] + t
                        if m >= 1:
                            s_bmm.w(v, m)
                        v.tensor_mul(qsq[:, :], kTt[:, t, :], kTt[:, t, :]).then_inc(s_bsq.h, 1)
                        s_bsq.n += 1
                bsqc[0] += KD

                gg = grp[0]
                grp[0] += 1
                bsq_base = s_bsq.n - KD

                @block.tensor
                def _(t_, c=c, gg=gg, bsq_base=bsq_base):
                    if gg >= 1:
                        s_bmax.w(t_, 2 * gg)
                    for t in range(KD):
                        s_bsq.w(t_, bsq_base + t + 1)
                        mm = t_.matmul(ps[0:1, 6, :], onec[:, :], qsq[:, :],
                                       start=(t == 0), stop=(t == KD - 1))
                        s_bmm.inc(mm)

                @block.vector
                def _(v, c=c, half=half):
                    s_bmm.w(v)
                    s_bmax.w(v)
                    v.reduce_max(srow[0:1, 0:1], ps[0:1, 6, :], axis=AX).then_inc(s_bmax.h, 1)
                    s_bmax.n += 1
                    s_bmax.w(v)
                    if half == 0 and c == 0:
                        v.tensor_copy(km[:, :], srow[0:1, 0:1]).then_inc(s_bmax.h, 1)
                    else:
                        v.tensor_tensor(km[:, :], km[:, :], srow[0:1, 0:1], ALU.max).then_inc(s_bmax.h, 1)
                    s_bmax.n += 1

            vgbase = pg[0]
            pg[0] += 32
            ccs_prev = s_ccs.n

            @block.tensor
            def _(t_, half=half, vgbase=vgbase):
                t_.wait_ge(s_wld.h, wld_v)
                for gi in range(32):
                    gidx = vgbase + gi
                    s_pev.w(t_, gidx - 1)
                    bank = gidx % 2
                    jt, vc = gi // 2, gi % 2
                    mm = None
                    for t in range(KD):
                        mm = t_.matmul(ps[:, bank, :], xn[:, t, jt * 128:(jt + 1) * 128],
                                       wbuf[:, t, vc * 512:(vc + 1) * 512],
                                       start=(t == 0), stop=(t == KD - 1))
                    s_pmm.inc(mm)

            @block.scalar
            def _(s, half=half, vgbase=vgbase, ccs_prev=ccs_prev):
                if half == 1:
                    s.wait_ge(s_ccs.h, ccs_prev)
                for gi in range(32):
                    gidx = vgbase + gi
                    s_pmm.w(s, gidx + 1)
                    jt, vc = gi // 2, gi % 2
                    s.copy(vtm[:, jt, vc * 512:(vc + 1) * 512], ps[:, gidx % 2, :]).then_inc(s_pev.h, 1)
                    s_pev.n += 1

            @block.sync
            def _(sy, half=half):
                sy.wait_ge(s_pev.h, s_pev.n)
                sy.wait_ge(s_ones.h, 3)
                sy.dma_start(cvo[half, :, :, 0:1025],
                             vtm[:, :, 0:1025]).then_inc(s_ccs.h, 16)
                s_ccs.n += 16

        emit_ln_chunks(0)

        @block.vector
        def _(v):
            v.memset(vtm[:, :, 1024:1025], 1.0).then_inc(s_ones.h, 1)
            s_ones.n += 1

        @block.tensor
        def _(t_):
            t_.wait_ge(s_wld.h, 16)
            s_vst.w(t_)
            s_ones.w(t_, 3)
            for gi in range(32):
                if gi >= 2:
                    s_pev.w(t_, gi - 1)
                bank = gi % 2
                c, o = gi // KD, gi % KD
                mm = None
                for t in range(KD):
                    mm = t_.matmul(ps[:, bank, :], wbuf[:, t, o * 128:(o + 1) * 128],
                                   xn[:, t, c * CH:(c + 1) * CH],
                                   start=(t == 0), stop=(t == KD - 1))
                s_pmm.inc(mm)
        pg[0] += 32

        @block.scalar
        def _(s):
            for gi in range(32):
                s_pmm.w(s, gi + 1)
                c, o = gi // KD, gi % KD
                s.copy(qT[:, o, c * CH:(c + 1) * CH], ps[:, gi % 2, :]).then_inc(s_pev.h, 1)
                s_pev.n += 1

        emit_kv(0)

        for c in range(NCH):
            gbase = pg[0]
            pg[0] += KD

            @block.tensor
            def _(t_, c=c, gbase=gbase):
                for o in range(KD):
                    gidx = gbase + o
                    s_pev.w(t_, gidx - 1)
                    bank = gidx % 2
                    mm = None
                    for t in range(KD):
                        mm = t_.matmul(ps[:, bank, :], wbuf[:, t, o * 128:(o + 1) * 128],
                                       xn[:, t, c * CH:(c + 1) * CH],
                                       start=(t == 0), stop=(t == KD - 1))
                    s_pmm.inc(mm)

            @block.scalar
            def _(s, c=c, gbase=gbase):
                for o in range(KD):
                    gidx = gbase + o
                    s_pmm.w(s, gidx + 1)
                    s.copy(vT[:, o, c * CH:(c + 1) * CH], ps[:, gidx % 2, :]).then_inc(s_pev.h, 1)
                    s_pev.n += 1

        for c in range(NCH):
            @block.vector
            def _(v, c=c):
                if c == 0:
                    s_pev.w(v, 32)
                for t in range(KD):
                    m = bsqc[0] + t
                    s_bmm.w(v, m)
                    v.tensor_mul(qsq[:, :], qT[:, t, c * CH:(c + 1) * CH],
                                 qT[:, t, c * CH:(c + 1) * CH]).then_inc(s_bsq.h, 1)
                    s_bsq.n += 1
            bsqc[0] += KD

            gg = grp[0]
            grp[0] += 1
            bsq_base = s_bsq.n - KD

            @block.tensor
            def _(t_, c=c, gg=gg, bsq_base=bsq_base):
                s_bmax.w(t_, 2 * gg)
                for t in range(KD):
                    s_bsq.w(t_, bsq_base + t + 1)
                    mm = t_.matmul(ps[0:1, 6, :], onec[:, :], qsq[:, :],
                                   start=(t == 0), stop=(t == KD - 1))
                    s_bmm.inc(mm)

            @block.vector
            def _(v, c=c):
                s_bmm.w(v)
                s_bmax.w(v)
                v.reduce_max(srow[0:1, 0:1], ps[0:1, 6, :], axis=AX).then_inc(s_bmax.h, 1)
                s_bmax.n += 1
                s_bmax.w(v)
                if c == 0:
                    v.tensor_copy(qm[:, :], srow[0:1, 0:1]).then_inc(s_bmax.h, 1)
                else:
                    v.tensor_tensor(qm[:, :], qm[:, :], srow[0:1, 0:1], ALU.max).then_inc(s_bmax.h, 1)
                s_bmax.n += 1

        emit_ln_chunks(1)
        emit_kv(1)

        @block.vector
        def _(v):
            s_bmax.w(v)
            v.tensor_add(srow[0:1, 0:1], qm[:, :], km[:, :]).then_inc(s_bmax.h, 1)
            s_bmax.n += 1
            s_bmax.w(v)
            v.tensor_scalar(srow[0:1, 0:1], srow[0:1, 0:1], -0.5 * SCALE, None, ALU.mult).then_inc(s_bmax.h, 1)
            s_bmax.n += 1

        @block.tensor
        def _(t_):
            s_bmax.w(t_)
            t_.matmul(ps[:, 4, 0:1], oner[0:1, :], srow[0:1, 0:1], start=True, stop=True).then_inc(s_bc.h, 1)
            s_bc.n += 1

        @block.vector
        def _(v):
            s_bc.w(v)
            v.tensor_copy(nB[:, :], ps[:, 4, 0:1]).then_inc(s_bmax.h, 1)
            s_bmax.n += 1

    PMMF = s_pmm.n
    BMMF = s_bmm.n
    CCSF = s_ccs.n
    VSTF = s_vst.n

    kstrip = sb("kstrip", [128, 2, KD, CH], BF16)
    vstrip = sb("vstrip", [128, 2, 4, 1025], BF16)
    mstrip = sb("mstrip", [128, 2, 4, CH], BF16)
    av_acc = sb("av_acc", [128, KD, CH], F32)
    e_sb = sb("e_sb", [128, 2, 4, CH], BF16)
    xn2 = sb("xn2", [128, KD, CH], BF16)
    zsq1 = sb("zsq1", [128, CH], F32)

    BMAXF = s_bmax.n

    for c in range(NCH):
        ep_z_end_prev = s_ep.n

        @block.vector
        def _(v, c=c):
            if c == 0:
                v.wait_ge(s_pmm.h, PMMF)
                v.wait_ge(s_bmm.h, BMMF)
                v.wait_ge(s_ccs.h, CCSF)
            s_ep.w(v)
            v.memset(av_acc[:, :, :], 0.0).then_inc(s_avac.h, 1)
            s_avac.n += 1

        avac_base = s_avac.n
        avmm_base = s_avmm.n

        for sidx in range(NS):
            gsi = c * NS + sidx
            slot = gsi % 2
            rank = sidx // 4

            @block.gpsimd
            def _(g, c=c, sidx=sidx, gsi=gsi, slot=slot, rank=rank):
                if gsi <= 1:
                    g.wait_ge(s_pmm.h, PMMF)
                    g.wait_ge(s_bmm.h, BMMF)
                    g.wait_ge(s_ccs.h, CCSF)
                if gsi >= 2:
                    g.wait_ge(s_smm.h, 4 * (gsi - 1))
                g.dma_start(kstrip[:, slot, :, :],
                            cko[rank, :, :, (sidx % 4) * CH:(sidx % 4 + 1) * CH]).then_inc(s_kld[slot].h, 16)
                s_kld[slot].n += 16
                if gsi >= 2:
                    g.wait_ge(s_avmm.h, 2 * (gsi - 1))
                g.dma_start(vstrip[:, slot, :, :],
                            cvo[rank, :, 4 * (sidx % 4):4 * (sidx % 4) + 4, 0:1025]).then_inc(s_vld[slot].h, 16)
                s_vld[slot].n += 16
                if gsi >= 2:
                    g.wait_ge(s_madd.h, 4 * (gsi - 1))
                g.dma_start(mstrip[:, slot, :, :], msk4[gsi, :, :, :]).then_inc(s_mld[slot].h, 16)
                s_mld[slot].n += 16

            @block.tensor
            def _(t_, c=c, gsi=gsi, slot=slot):
                t_.wait_ge(s_kld[slot].h, 16 * (gsi // 2 + 1))
                if gsi == 0:
                    t_.wait_ge(s_bmax.h, BMAXF)
                for jt in range(4):
                    idx = 4 * gsi + jt
                    bank = 5 + idx % 2
                    if idx >= 2:
                        s_exp.w(t_, idx - 1)
                    elif c > 0:
                        pass
                    mm = None
                    for t in range(KD):
                        mm = t_.matmul(ps[:, bank, :], kstrip[:, slot, t, jt * 128:(jt + 1) * 128],
                                       qT[:, t, c * CH:(c + 1) * CH],
                                       start=(t == 0), stop=(t == KD - 1))
                    s_smm.inc(mm)

            @block.vector
            def _(v, gsi=gsi, slot=slot):
                v.wait_ge(s_mld[slot].h, 16 * (gsi // 2 + 1))
                for jt in range(4):
                    idx = 4 * gsi + jt
                    bank = 5 + idx % 2
                    s_smm.w(v, idx + 1)
                    v.tensor_tensor(ps[:, bank, :], ps[:, bank, :], mstrip[:, slot, jt, :], ALU.add).then_inc(s_madd.h, 1)
                    s_madd.n += 1

            @block.scalar
            def _(s, gsi=gsi, slot=slot):
                if gsi == 0:
                    s.wait_ge(s_bmax.h, BMAXF)
                if gsi >= 2:
                    s.wait_ge(s_avmm.h, 2 * (gsi - 1))
                for jt in range(4):
                    idx = 4 * gsi + jt
                    bank = 5 + idx % 2
                    s_madd.w(s, idx + 1)
                    s.activation(e_sb[:, slot, jt, :], ps[:, bank, :], AF.Exp,
                                 bias=nB[:, 0:1], scale=SCALE).then_inc(s_exp.h, 1)
                    s_exp.n += 1

            @block.tensor
            def _(t_, c=c, gsi=gsi, slot=slot, sidx=sidx, avac_base=avac_base,
                  ep_z_end_prev=ep_z_end_prev):
                t_.wait_ge(s_exp.h, 4 * (gsi + 1))
                t_.wait_ge(s_vld[slot].h, 16 * (gsi // 2 + 1))
                if sidx == 0:
                    t_.wait_ge(s_ep.h, ep_z_end_prev)
                    t_.wait_ge(s_avac.h, avac_base)
                for half in range(2):
                    need = avac_base + 4 * (2 * sidx + half)
                    t_.wait_ge(s_avac.h, max(need, avac_base))
                    last_av = None
                    for m in range(4):
                        for jt in range(4):
                            last_av = t_.matmul(ps[:, m, :],
                                                vstrip[:, slot, jt, (half * 4 + m) * 128:(half * 4 + m + 1) * 128],
                                                e_sb[:, slot, jt, :],
                                                start=(jt == 0), stop=(jt == 3))
                    if half == 0:
                        last_av.then_inc(s_avmm.h, 1)
                        s_avmm.n += 1
                    else:
                        pass
                mm = None
                for jt in range(4):
                    mm = t_.matmul(ps[0:1, 4, :], vstrip[:, slot, jt, 1024:1025],
                                   e_sb[:, slot, jt, :],
                                   start=(sidx == 0 and jt == 0), stop=(sidx == NS - 1 and jt == 3))
                mm.then_inc(s_avmm.h, 1)
                s_avmm.n += 1
                if sidx == NS - 1:
                    mm.then_inc(s_sumc.h, 0) if False else None

            @block.vector
            def _(v, gsi=gsi, sidx=sidx, avmm_base=avmm_base):
                for half in range(2):
                    v.wait_ge(s_avmm.h, avmm_base + 2 * sidx + half + 1)
                    for m in range(4):
                        s_avac.w(v)
                        v.tensor_add(av_acc[:, half * 4 + m, :], av_acc[:, half * 4 + m, :],
                                     ps[:, m, :]).then_inc(s_avac.h, 1)
                        s_avac.n += 1

        AVMM_C = s_avmm.n

        @block.vector
        def _(v, c=c):
            v.wait_ge(s_avmm.h, AVMM_C)
            v.tensor_copy(srow[0:1, :], ps[0:1, 4, :]).then_inc(s_ep.h, 1)
            s_ep.n += 1
            s_ep.w(v)
            v.reciprocal(rrow[:, :], srow[0:1, :]).then_inc(s_ep.h, 1)
            s_ep.n += 1

        @block.tensor
        def _(t_, c=c):
            s_ep.w(t_)
            t_.matmul(ps[:, 4, :], oner[0:1, :], rrow[:, :], start=True, stop=True).then_inc(s_bc.h, 1)
            s_bc.n += 1

        @block.vector
        def _(v, c=c):
            s_bc.w(v)
            s_avac.w(v)
            for t in range(KD):
                s_ep.w(v)
                v.tensor_mul(ztmp[:, :], av_acc[:, t, :], ps[:, 4, :]).then_inc(s_ep.h, 1)
                s_ep.n += 1
                s_ep.w(v)
                v.tensor_add(av_acc[:, t, :], ztmp[:, :], vT[:, t, c * CH:(c + 1) * CH]).then_inc(s_ep.h, 1)
                s_ep.n += 1
        EPZ = s_ep.n

        @block.vector
        def _(v, c=c):
            for t in range(KD):
                m = c * KD + t
                if m >= 1:
                    v.wait_ge(s_zmm.h, m)
                v.tensor_mul(zsq1[:, :], av_acc[:, t, :], av_acc[:, t, :]).then_inc(s_zsq.h, 1)
                s_zsq.n += 1

        @block.tensor
        def _(t_, c=c):
            s_ep.w(t_, EPZ)
            s_exp.w(t_, 4 * NS * (c + 1))
            for t in range(KD):
                t_.matmul(ps[0:1, 6, :], onec[:, :], av_acc[:, t, :],
                          start=(t == 0), stop=(t == KD - 1))
            mm = None
            for t in range(KD):
                s_zsq.w(t_, c * KD + t + 1)
                mm = t_.matmul(ps[0:1, 5, :], onec[:, :], zsq1[:, :],
                               start=(t == 0), stop=(t == KD - 1))
                s_zmm.inc(mm)

        @block.vector
        def _(v, c=c):
            s_zmm.w(v)
            v.tensor_scalar(mrow[:, :], ps[0:1, 6, :], 1.0 / D, None, ALU.mult).then_inc(s_ep.h, 1)
            s_ep.n += 1
            v.tensor_scalar(msrow[:, :], ps[0:1, 5, :], 1.0 / D, None, ALU.mult).then_inc(s_ep.h, 1)
            s_ep.n += 1
            s_ep.w(v)
            v.scalar_tensor_tensor(vrow[:, :], mrow[:, :], -1.0, mrow[:, :], ALU.mult, ALU.mult).then_inc(s_ep.h, 1)
            s_ep.n += 1
            s_ep.w(v)
            v.tensor_add(vrow[:, :], vrow[:, :], msrow[:, :]).then_inc(s_ep.h, 1)
            s_ep.n += 1
            s_ep.w(v)
            v.tensor_scalar(vrow[:, :], vrow[:, :], EPS, None, ALU.add).then_inc(s_ep.h, 1)
            s_ep.n += 1

        @block.scalar
        def _(s, c=c):
            s_ep.w(s)
            s.activation(lnv[:, :], vrow[:, :], AF.Ln).then_inc(s_sst.h, 1)
            s_sst.n += 1
            s_sst.w(s)
            s.activation(rstd[:, :], lnv[:, :], AF.Exp, scale=-0.5).then_inc(s_sst.h, 1)
            s_sst.n += 1

        @block.vector
        def _(v, c=c):
            s_sst.w(v)
            v.scalar_tensor_tensor(acoef[:, :], mrow[:, :], -1.0, rstd[:, :], ALU.mult, ALU.mult).then_inc(s_ep.h, 1)
            s_ep.n += 1

        @block.tensor
        def _(t_, c=c):
            s_ep.w(t_)
            t_.matmul(ps[:, 3, :], oner[0:1, :], rstd[:, :], start=True, stop=True).then_inc(s_bc.h, 1)
            t_.matmul(ps[:, 2, :], oner[0:1, :], acoef[:, :], start=True, stop=True).then_inc(s_bc.h, 1)
            s_bc.n += 2

        @block.vector
        def _(v, c=c):
            s_bc.w(v)
            if c == 0:
                s_cst.w(v, 48)
            for t in range(KD):
                s_ep.w(v)
                v.tensor_mul(ztmp[:, :], av_acc[:, t, :], ps[:, 3, :]).then_inc(s_ep.h, 1)
                s_ep.n += 1
                s_ep.w(v)
                v.tensor_add(tmpv[:, :], ztmp[:, :], ps[:, 2, :]).then_inc(s_ep.h, 1)
                s_ep.n += 1
                s_ep.w(v)
                v.tensor_scalar(xn2[:, t, :], tmpv[:, :], gam[:, t:t + 1], bet[:, t:t + 1],
                                ALU.mult, ALU.add).then_inc(s_ep.h, 1)
                s_ep.n += 1

        XN2 = s_ep.n

        @block.sync
        def _(sy, c=c):
            if c == 0:
                sy.wait_ge(s_vst.h, VSTF)
            else:
                sy.wait_ge(s_out.h, 16 * c)
            sy.dma_start(xr[:, :, :], x3[:, :, c * CH:(c + 1) * CH]).then_inc(s_xres.h, 16)
            s_xres.n += 16

        @block.tensor
        def _(t_, c=c):
            t_.wait_ge(s_wld.h, 96)
            s_ep.w(t_, XN2)
            for o in range(KD):
                gidx = c * KD + o
                if gidx >= 2:
                    s_res.w(t_, gidx - 1)
                bank = gidx % 2
                mm = None
                for t in range(KD):
                    mm = t_.matmul(ps[:, bank, :], wbuf[:, t, o * 128:(o + 1) * 128],
                                   xn2[:, t, :], start=(t == 0), stop=(t == KD - 1))
                s_fcmm.inc(mm)

        @block.vector
        def _(v, c=c):
            s_xres.w(v)
            for o in range(KD):
                gidx = c * KD + o
                s_fcmm.w(v, gidx + 1)
                bank = gidx % 2
                v.scalar_tensor_tensor(xr[:, o, :], ps[:, bank, :], bfc[:, o:o + 1],
                                       xr[:, o, :], ALU.add, ALU.add).then_inc(s_res.h, 1)
                s_res.n += 1

        @block.sync
        def _(sy, c=c):
            sy.wait_ge(s_res.h, s_res.n)
            sy.dma_start(xo3[:, :, c * CH:(c + 1) * CH], xr[:, :, :]).then_inc(s_out.h, 16)
            s_out.n += 16

    @block.sync
    def _(sy):
        sy.wait_ge(s_out.h, 16 * NCH)


def build_l0attn_program(T):
    nc = bass.Bass("TRN2", target_bir_lowering=False)
    NJT = T // 128
    io = {
        "xT_in": nc.dram_tensor("xT_in", [128, KD * T], F32, kind="ExternalInput"),
        "xT_other": nc.dram_tensor("xT_other", [128, KD * T], F32, kind="ExternalInput"),
        "xT_out": nc.dram_tensor("xT_out", [128, KD * T], F32, kind="ExternalOutput"),
        "Wq": nc.dram_tensor("Wq", [D, D], F32, kind="ExternalInput"),
        "Wk": nc.dram_tensor("Wk", [D, D], F32, kind="ExternalInput"),
        "Wv": nc.dram_tensor("Wv", [D, D], F32, kind="ExternalInput"),
        "Wfc": nc.dram_tensor("Wfc", [D, D], F32, kind="ExternalInput"),
        "gamr": nc.dram_tensor("gamr", [128, KD], F32, kind="ExternalInput"),
        "betr": nc.dram_tensor("betr", [128, KD], F32, kind="ExternalInput"),
        "bfcr": nc.dram_tensor("bfcr", [128, KD], F32, kind="ExternalInput"),
        "identf": nc.dram_tensor("identf", [128, 128], F32, kind="ExternalInput"),
        "maskstream": nc.dram_tensor("maskstream", [(T // CH) * (2 * T // CH), 128, 2048], BF16, kind="ExternalInput"),
        "cck_out": nc.dram_tensor("cck_out", [256, KD * T], BF16),
        "ccv_out": nc.dram_tensor("ccv_out", [2 * T, 1025], BF16),
    }
    with contextlib.ExitStack() as ctx:
        block = ctx.enter_context(nc.Block())
        emit_l0attn_sublayer(nc, ctx, block, T, io)
    return nc


def l0attn_host_inputs(T, core_id, x_slice, Wq, Wk, Wv, Wfc, gamma, beta, bfc_, x_other=None):
    xT = np.ascontiguousarray(x_slice.reshape(T, KD, 128).transpose(2, 1, 0).reshape(128, KD * T))
    if x_other is None:
        xTo = xT
    else:
        xTo = np.ascontiguousarray(x_other.reshape(T, KD, 128).transpose(2, 1, 0).reshape(128, KD * T))
    par = core_id % 2
    NCH = T // CH
    NS = 2 * T // CH
    mask = np.zeros((NCH * NS, 128, 2048), dtype=np.float32)
    col = np.arange(CH)
    for c in range(NCH):
        for sidx in range(NS):
            for jt in range(4):
                kk = sidx * CH + jt * 128 + np.arange(128)[:, None]
                jg = np.where(kk < T, par * T + kk, (1 - par) * T + (kk - T))
                qg = par * T + c * CH + col[None, :]
                mask[c * NS + sidx, :, jt * CH:(jt + 1) * CH] = np.where(jg > qg, -1e30, 0.0)
    return {
        "xT_in": xT,
        "xT_other": xTo,
        "Wq": np.ascontiguousarray(Wq), "Wk": np.ascontiguousarray(Wk),
        "Wv": np.ascontiguousarray(Wv), "Wfc": np.ascontiguousarray(Wfc),
        "gamr": np.ascontiguousarray(gamma.reshape(KD, 128).T),
        "betr": np.ascontiguousarray(beta.reshape(KD, 128).T),
        "bfcr": np.ascontiguousarray(bfc_.reshape(KD, 128).T),
        "identf": np.eye(128, dtype=np.float32),
        "maskstream": mask.astype(ml_dtypes.bfloat16),
    }



def emit_l1attn_sublayer(nc, ctx, block, T, io, tag="a1"):
    NCH = T // CH
    NJT2 = 2 * T // 128
    NH = 16
    SCALE = 1.0 / 8.0

    sb = lambda name, shape, dt: ctx.enter_context(nc.sbuf_tensor(name + tag, shape, dt))
    qT = sb("qT", [128, KD, T], BF16)
    wbuf = sb("wbuf", [128, KD, D], BF16)
    xr = sb("xr", [128, KD, CH], F32)
    mrow = sb("mrow", [1, CH], F32)
    msrow = sb("msrow", [1, CH], F32)
    lnv = msrow
    vrow = sb("vrow", [1, CH], F32)
    rstd = sb("rstd", [1, CH], F32)
    acoef = sb("acoef", [1, CH], F32)
    tmpv = sb("tmpv", [128, CH], F32)
    srow = tmpv
    idb = sb("idb", [128, 128], BF16)
    onec = sb("onec", [128, 1], F32)
    oner = sb("oner", [1, 128], F32)
    bo = sb("bo", [128, KD], F32)
    qmp = sb("qmp", [1, KD], F32)
    kmp = sb("kmp", [1, KD], F32)
    kmpg = sb("kmpg", [1, KD], F32)
    nBp = sb("nBp", [128, KD], F32)
    nBc = sb("nBc", [128, 2], F32)
    rrow = vrow
    rb_sb = sb("rb_sb", [128, CH], F32)
    e_sb = sb("e_sb", [128, 2, CH], BF16)
    a2 = sb("a2", [128, NH, CH], F32)
    a2p = sb("a2p", [128, KD, CH], BF16)

    ps = ctx.enter_context(nc.psum_tensor("ps" + tag, [128, 8, 512], F32))

    S = lambda name: Sem(ctx, nc, name + tag)
    s_cst, s_wld, s_ones, s_idb = S("s_cst"), S("s_wld"), S("s_ones"), S("s_idb")
    s_x, s_xsqv, s_sqmm = S("s_x"), S("s_xsqv"), S("s_sqmm")
    s_vst, s_sst, s_rbab = S("s_vst"), S("s_sst"), S("s_rbab")
    s_pmm, s_pev = S("s_pmm"), S("s_pev")
    s_bsq, s_bmm, s_bmax = S("s_bsq"), S("s_bmm"), S("s_bmax")
    s_ccs, s_cck, s_ccv, s_ccm, s_kml = S("s_ccs"), S("s_cck"), S("s_ccv"), S("s_ccm"), S("s_kml")
    s_kld = [S("s_klda"), S("s_kldb")]
    s_vld = [S("s_vlda"), S("s_vldb")]
    s_smm, s_exp, s_avmm = S("s_smm"), S("s_exp"), S("s_avmm")
    s_epv, s_bc, s_ppk, s_omm = S("s_epv"), S("s_bc"), S("s_ppk"), S("s_omm")
    s_xres, s_res, s_out, s_nbc = S("s_xres"), S("s_res"), S("s_out"), S("s_nbc")

    x3 = io["xT_in"].rearrange("p (t j) -> p t j", t=KD)
    xo3 = io["xT_out"].rearrange("p (t j) -> p t j", t=KD)
    cko = io["cck_out"].rearrange("(r p) (t j) -> r p t j", p=128, t=KD)
    cvo = io["ccv_out"].rearrange("(r j) v -> r j v", r=2)

    @block.sync
    def _(sy):
        sy.dma_start(bo[:, :], io["bor"][:, :]).then_inc(s_cst.h, 16)
        s_cst.n += 16

    @block.gpsimd
    def _(g):
        g.dma_start(idb[:, :], io["identf"][:, :]).then_inc(s_idb.h, 16)
        s_idb.n += 16
        g.memset(onec[:, :], 1.0).then_inc(s_ones.h, 1)
        g.memset(oner[:, :], 1.0).then_inc(s_ones.h, 1)
        s_ones.n += 2
        for wname, thr in [("Wq", 0), ("Wk", 32), ("Wv", 64),
                           ("Wk", 96), ("Wv", 128)]:
            if thr:
                g.wait_ge(s_pmm.h, thr)
            g.dma_start(wbuf[:, :, :], io[wname].rearrange("(t p) o -> p t o", p=128)).then_inc(s_wld.h, 16)
            s_wld.n += 16

    with contextlib.ExitStack() as pctx:
        xn = pctx.enter_context(nc.sbuf_tensor("xn" + tag, [128, KD, T], BF16))
        kTt = pctx.enter_context(nc.sbuf_tensor("kTt" + tag, [128, KD, CH], BF16))
        vtm = pctx.enter_context(nc.sbuf_tensor("vtm" + tag, [128, T // 128, 1040], BF16))
        xsq = pctx.enter_context(nc.sbuf_tensor("xsq" + tag, [128, CH], F32))
        qsq = xsq

        xoth3 = io["xT_other"].rearrange("p (t j) -> p t j", t=KD)
        pg = [0]
        gmc = [0]

        def emit_ln_chunks(half):
            x3h = x3 if half == 0 else xoth3
            for c in range(NCH):
                cg = half * NCH + c

                @block.sync
                def _(sy, c=c, cg=cg, x3h=x3h):
                    if cg > 0:
                        sy.wait_ge(s_vst.h, 22 * cg)
                    sy.dma_start(xr[:, :, :], x3h[:, :, c * CH:(c + 1) * CH]).then_inc(s_x.h, 16)
                    s_x.n += 16

                @block.vector
                def _(v, c=c, cg=cg):
                    s_x.w(v)
                    if cg == NCH:
                        s_bmm.w(v)
                    for t in range(KD):
                        m = cg * KD + t
                        if m >= 1:
                            s_sqmm.w(v, m)
                        v.tensor_mul(xsq[:, :], xr[:, t, :], xr[:, t, :]).then_inc(s_xsqv.h, 1)
                        s_xsqv.n += 1

                @block.tensor
                def _(t_, c=c, cg=cg):
                    s_ones.w(t_, 2)
                    s_x.w(t_)
                    if cg > 0:
                        s_vst.w(t_, 22 * (cg - 1) + 2)
                    if cg == NCH:
                        s_bmax.w(t_)
                    for t in range(KD):
                        t_.matmul(ps[0:1, 6, :], onec[:, :], xr[:, t, :],
                                  start=(t == 0), stop=(t == KD - 1))
                    for t in range(KD):
                        s_xsqv.w(t_, cg * KD + t + 1)
                        mm = t_.matmul(ps[0:1, 5, :], onec[:, :], xsq[:, :],
                                       start=(t == 0), stop=(t == KD - 1))
                        s_sqmm.inc(mm)

                @block.vector
                def _(v, c=c):
                    s_sqmm.w(v)
                    v.tensor_scalar(mrow[:, :], ps[0:1, 6, :], 1.0 / D, None, ALU.mult).then_inc(s_vst.h, 1)
                    v.tensor_scalar(msrow[:, :], ps[0:1, 5, :], 1.0 / D, None, ALU.mult).then_inc(s_vst.h, 1)
                    s_vst.n += 2
                    s_vst.w(v)
                    v.scalar_tensor_tensor(vrow[:, :], mrow[:, :], -1.0, mrow[:, :], ALU.mult, ALU.mult).then_inc(s_vst.h, 1)
                    s_vst.n += 1
                    s_vst.w(v)
                    v.tensor_add(vrow[:, :], vrow[:, :], msrow[:, :]).then_inc(s_vst.h, 1)
                    s_vst.n += 1
                    s_vst.w(v)
                    v.tensor_scalar(vrow[:, :], vrow[:, :], EPS, None, ALU.add).then_inc(s_vst.h, 1)
                    s_vst.n += 1

                @block.scalar
                def _(s, c=c):
                    s_vst.w(s)
                    s.activation(lnv[:, :], vrow[:, :], AF.Ln).then_inc(s_sst.h, 1)
                    s_sst.n += 1
                    s_sst.w(s)
                    s.activation(rstd[:, :], lnv[:, :], AF.Exp, scale=-0.5).then_inc(s_sst.h, 1)
                    s_sst.n += 1

                @block.vector
                def _(v, c=c):
                    s_sst.w(v)
                    v.scalar_tensor_tensor(acoef[:, :], mrow[:, :], -1.0, rstd[:, :], ALU.mult, ALU.mult).then_inc(s_vst.h, 1)
                    s_vst.n += 1

                @block.tensor
                def _(t_, c=c):
                    s_vst.w(t_)
                    t_.matmul(ps[:, 3, :], oner[0:1, :], rstd[:, :], start=True, stop=True).then_inc(s_rbab.h, 1)
                    t_.matmul(ps[:, 4, :], oner[0:1, :], acoef[:, :], start=True, stop=True).then_inc(s_rbab.h, 1)
                    s_rbab.n += 2

                @block.vector
                def _(v, c=c, cg=cg):
                    s_rbab.w(v)
                    if cg == NCH:
                        s_pmm.w(v)
                    for t in range(KD):
                        s_vst.w(v)
                        v.tensor_mul(tmpv[:, :], xr[:, t, :], ps[:, 3, :]).then_inc(s_vst.h, 1)
                        s_vst.n += 1
                        s_vst.w(v)
                        v.tensor_add(xn[:, t, c * CH:(c + 1) * CH], tmpv[:, :], ps[:, 4, :]).then_inc(s_vst.h, 1)
                        s_vst.n += 1

        def emit_kv(half):
            wld_k = 32 if half == 0 else 64
            wld_v = 48 if half == 0 else 80
            vst_full = s_vst.n

            for c in range(NCH):
                gbase = pg[0]
                pg[0] += KD

                @block.tensor
                def _(t_, c=c, half=half, gbase=gbase, vst_full=vst_full, wld_k=wld_k):
                    if c == 0:
                        t_.wait_ge(s_wld.h, wld_k)
                        if half == 1:
                            s_vst.w(t_, vst_full)
                    for o in range(KD):
                        gidx = gbase + o
                        s_pev.w(t_, gidx - 1)
                        mm = None
                        for t in range(KD):
                            mm = t_.matmul(ps[:, gidx % 2, :], wbuf[:, t, o * 128:(o + 1) * 128],
                                           xn[:, t, c * CH:(c + 1) * CH],
                                           start=(t == 0), stop=(t == KD - 1))
                        s_pmm.inc(mm)

                first_k = (half == 0 and c == 0)
                ccs_pre = s_ccs.n
                bsq_pre = s_bsq.n

                @block.scalar
                def _(s, c=c, gbase=gbase, first_k=first_k, ccs_pre=ccs_pre, bsq_pre=bsq_pre):
                    if not first_k:
                        s.wait_ge(s_ccs.h, ccs_pre)
                        s.wait_ge(s_bsq.h, bsq_pre)
                    for o in range(KD):
                        gidx = gbase + o
                        s_pmm.w(s, gidx + 1)
                        s.copy(kTt[:, o, :], ps[:, gidx % 2, :]).then_inc(s_pev.h, 1)
                        s_pev.n += 1

                @block.sync
                def _(sy, c=c, half=half):
                    sy.wait_ge(s_pev.h, s_pev.n)
                    sy.dma_start(cko[half, :, :, c * CH:(c + 1) * CH],
                                 kTt[:, :, :]).then_inc(s_ccs.h, 16)
                    s_ccs.n += 16

                gmb = gmc[0]
                gmc[0] += KD

                @block.vector
                def _(v, c=c, half=half, gmb=gmb):
                    s_pev.w(v)
                    for t in range(KD):
                        m = gmb + t
                        if m >= 1:
                            s_bmm.w(v, m)
                        v.tensor_mul(qsq[:, :], kTt[:, t, :], kTt[:, t, :]).then_inc(s_bsq.h, 1)
                        s_bsq.n += 1
                        s_bmm.w(v, m + 1)
                        s_bmax.w(v)
                        v.reduce_max(srow[0:1, 0:1], ps[0:1, 6, :], axis=AX).then_inc(s_bmax.h, 1)
                        s_bmax.n += 1
                        s_bmax.w(v)
                        if half == 0 and c == 0:
                            v.tensor_copy(kmp[0:1, t:t + 1], srow[0:1, 0:1]).then_inc(s_bmax.h, 1)
                        else:
                            v.tensor_tensor(kmp[0:1, t:t + 1], kmp[0:1, t:t + 1], srow[0:1, 0:1], ALU.max).then_inc(s_bmax.h, 1)
                        s_bmax.n += 1

                @block.tensor
                def _(t_, c=c, gmb=gmb):
                    for t in range(KD):
                        m = gmb + t
                        if m >= 1:
                            s_bmax.w(t_, 2 * m)
                        s_bsq.w(t_, m + 1)
                        mm = t_.matmul(ps[0:1, 6, :], onec[:, :], qsq[:, :],
                                       start=True, stop=True)
                        s_bmm.inc(mm)

            vgbase = pg[0]
            pg[0] += 32
            ccs_prev = s_ccs.n

            @block.tensor
            def _(t_, half=half, vgbase=vgbase, wld_v=wld_v):
                t_.wait_ge(s_wld.h, wld_v)
                for gi in range(32):
                    gidx = vgbase + gi
                    s_pev.w(t_, gidx - 1)
                    jt, vc = gi // 2, gi % 2
                    mm = None
                    for t in range(KD):
                        mm = t_.matmul(ps[:, gidx % 2, :], xn[:, t, jt * 128:(jt + 1) * 128],
                                       wbuf[:, t, vc * 512:(vc + 1) * 512],
                                       start=(t == 0), stop=(t == KD - 1))
                    s_pmm.inc(mm)

            @block.scalar
            def _(s, half=half, vgbase=vgbase, ccs_prev=ccs_prev):
                s_ones.w(s, 3)
                if half == 1:
                    s.wait_ge(s_ccs.h, ccs_prev)
                for gi in range(32):
                    gidx = vgbase + gi
                    s_pmm.w(s, gidx + 1)
                    jt, vc = gi // 2, gi % 2
                    for hh in range(8):
                        h = vc * 8 + hh
                        cp = s.copy(vtm[:, jt, h * 65:h * 65 + 64],
                                    ps[:, gidx % 2, hh * 64:(hh + 1) * 64])
                        if hh == 7:
                            cp.then_inc(s_pev.h, 1)
                            s_pev.n += 1

            if half == 0:
                @block.vector
                def _(v):
                    s_pev.w(v)
                    v.memset(vtm[:, :, :].rearrange("p jt (h u) -> p jt h u", h=NH)[:, :, :, 64:65], 1.0).then_inc(s_ones.h, 1)
                    s_ones.n += 1

            @block.sync
            def _(sy, half=half):
                sy.wait_ge(s_pev.h, s_pev.n)
                sy.wait_ge(s_ones.h, 4)
                sy.dma_start(cvo[half].rearrange("(jt p) v -> p jt v", p=128),
                             vtm[:, :, :]).then_inc(s_ccs.h, 16)
                s_ccs.n += 16

        emit_ln_chunks(0)

        @block.vector
        def _(v):
            v.memset(vtm[:, :, :], 0.0).then_inc(s_ones.h, 1)
            s_ones.n += 1

        @block.tensor
        def _(t_):
            t_.wait_ge(s_wld.h, 16)
            s_vst.w(t_)
            s_ones.w(t_, 2)
            for gi in range(32):
                if gi >= 2:
                    s_pev.w(t_, gi - 1)
                c, o = gi // KD, gi % KD
                mm = None
                for t in range(KD):
                    mm = t_.matmul(ps[:, gi % 2, :], wbuf[:, t, o * 128:(o + 1) * 128],
                                   xn[:, t, c * CH:(c + 1) * CH],
                                   start=(t == 0), stop=(t == KD - 1))
                s_pmm.inc(mm)
        pg[0] += 32

        @block.scalar
        def _(s):
            for gi in range(32):
                s_pmm.w(s, gi + 1)
                c, o = gi // KD, gi % KD
                s.copy(qT[:, o, c * CH:(c + 1) * CH], ps[:, gi % 2, :]).then_inc(s_pev.h, 1)
                s_pev.n += 1

        emit_kv(0)

        for c in range(NCH):
            gmb = gmc[0]
            gmc[0] += KD

            @block.vector
            def _(v, c=c, gmb=gmb):
                if c == 0:
                    s_pev.w(v, 32)
                for t in range(KD):
                    m = gmb + t
                    s_bmm.w(v, m)
                    v.tensor_mul(qsq[:, :], qT[:, t, c * CH:(c + 1) * CH],
                                 qT[:, t, c * CH:(c + 1) * CH]).then_inc(s_bsq.h, 1)
                    s_bsq.n += 1
                    s_bmm.w(v, m + 1)
                    s_bmax.w(v)
                    v.reduce_max(srow[0:1, 0:1], ps[0:1, 6, :], axis=AX).then_inc(s_bmax.h, 1)
                    s_bmax.n += 1
                    s_bmax.w(v)
                    if c == 0:
                        v.tensor_copy(qmp[0:1, t:t + 1], srow[0:1, 0:1]).then_inc(s_bmax.h, 1)
                    else:
                        v.tensor_tensor(qmp[0:1, t:t + 1], qmp[0:1, t:t + 1], srow[0:1, 0:1], ALU.max).then_inc(s_bmax.h, 1)
                    s_bmax.n += 1

            @block.tensor
            def _(t_, c=c, gmb=gmb):
                for t in range(KD):
                    m = gmb + t
                    s_bmax.w(t_, 2 * m)
                    s_bsq.w(t_, m + 1)
                    mm = t_.matmul(ps[0:1, 6, :], onec[:, :], qsq[:, :],
                                   start=True, stop=True)
                    s_bmm.inc(mm)

        emit_ln_chunks(1)
        emit_kv(1)

        @block.vector
        def _(v):
            s_bmax.w(v)
            v.tensor_add(srow[0:1, 0:KD], qmp[0:1, :], kmp[0:1, :]).then_inc(s_bmax.h, 1)
            s_bmax.n += 1
            s_bmax.w(v)
            v.tensor_scalar(srow[0:1, 0:KD], srow[0:1, 0:KD], -0.5 * SCALE, None, ALU.mult).then_inc(s_bmax.h, 1)
            s_bmax.n += 1

        @block.tensor
        def _(t_):
            s_bmax.w(t_)
            t_.matmul(ps[:, 4, 0:KD], oner[0:1, :], srow[0:1, 0:KD], start=True, stop=True).then_inc(s_bc.h, 1)
            s_bc.n += 1

        @block.vector
        def _(v):
            s_bc.w(v)
            v.tensor_copy(nBp[:, :], ps[:, 4, 0:KD]).then_inc(s_bmax.h, 1)
            s_bmax.n += 1

    BMAXF = s_bmax.n
    PMMF = s_pmm.n
    BMMF = s_bmm.n
    CCSF = s_ccs.n
    PEVF = s_pev.n
    VSTF = s_vst.n

    kzp = sb("kzp", [128, 2, 2 * T], BF16)
    vhb = sb("vhb", [128, 2, NJT2, 65], BF16)
    wo = sb("wo", [128, KD, D], BF16)
    id64a = sb("id64a", [128, 128], F32)
    id64b = sb("id64b", [128, 128], F32)
    rbs = sb("rbs", [128, CH], F32)

    @block.gpsimd
    def _(g):
        g.wait_ge(s_pmm.h, PMMF)
        g.wait_ge(s_bmm.h, BMMF)
        g.wait_ge(s_ccs.h, CCSF)
        g.memset(kzp[64:128, 0, :], 0.0).then_inc(s_ones.h, 1)
        g.memset(kzp[0:64, 1, :], 0.0).then_inc(s_ones.h, 1)
        s_ones.n += 2
        g.dma_start(wo[:, :, :], io["wor"].rearrange("(t p) o -> p t o", p=128)).then_inc(s_wld.h, 16)
        g.dma_start(id64a[:, :], io["id64a"][:, :]).then_inc(s_wld.h, 16)
        g.dma_start(id64b[:, :], io["id64b"][:, :]).then_inc(s_wld.h, 16)
        s_wld.n += 48

    for c in range(NCH):
        for h in range(NH):
            gh = c * NH + h
            par = h % 2
            pp = h // 2

            @block.gpsimd
            def _(g, c=c, h=h, gh=gh, par=par, pp=pp):
                if gh >= 2:
                    g.wait_ge(s_smm.h, NJT2 * (gh - 1))
                for r in range(2):
                    g.dma_start(kzp[par * 64:par * 64 + 64, par, r * T:(r + 1) * T],
                                cko[r, par * 64:par * 64 + 64, pp, :]).then_inc(s_kld[par].h, 16)
                    s_kld[par].n += 16
                if gh >= 2:
                    g.wait_ge(s_avmm.h, gh - 1)
                for r in range(2):
                    g.dma_start(vhb[:, par, r * (T // 128):(r + 1) * (T // 128), :],
                                cvo[r, :, h * 65:(h + 1) * 65].rearrange("(jt p) v -> p jt v", p=128)
                                ).then_inc(s_vld[par].h, 16)
                    s_vld[par].n += 16

            @block.tensor
            def _(t_, c=c, h=h, gh=gh, par=par, pp=pp):
                if gh == 0:
                    t_.wait_ge(s_bmax.h, BMAXF)
                    t_.wait_ge(s_wld.h, 128)
                    t_.wait_ge(s_ones.h, 6)
                    t_.wait_ge(s_pev.h, PEVF)
                t_.wait_ge(s_kld[par].h, 16 * (gh // 2 + 1) * 2)
                t_.wait_ge(s_vld[par].h, 16 * (gh // 2 + 1) * 2)
                t_.wait_ge(s_epv.h, max(3 * gh, 0))
                for g in range(NJT2):
                    idx = NJT2 * gh + g
                    if idx >= 2:
                        s_exp.w(t_, idx - 1)
                    mm = t_.matmul(ps[:, g % 2, :],
                                   kzp[:, par, g * 128:(g + 1) * 128],
                                   qT[:, pp, c * CH:(c + 1) * CH],
                                   start=True, stop=True)
                    s_smm.inc(mm)
                    s_exp.w(t_, idx + 1)
                    mm2 = t_.matmul(ps[0:65, 2, :],
                                    vhb[:, par, g, :],
                                    e_sb[:, g % 2, :],
                                    start=(g == 0), stop=(g == NJT2 - 1))
                if True:
                    mm2.then_inc(s_avmm.h, 1)
                    s_avmm.n += 1

            @block.scalar
            def _(s, c=c, h=h, gh=gh, par=par, pp=pp):
                if gh == 0:
                    s.wait_ge(s_bmax.h, BMAXF)
                for g in range(NJT2):
                    idx = NJT2 * gh + g
                    s_smm.w(s, idx + 1)
                    s.activation(e_sb[:, g % 2, :], ps[:, g % 2, :], AF.Exp,
                                 bias=nBp[:, pp:pp + 1], scale=SCALE).then_inc(s_exp.h, 1)
                    s_exp.n += 1

            @block.vector
            def _(v, c=c, h=h, gh=gh):
                v.wait_ge(s_avmm.h, gh + 1)
                v.tensor_copy(srow[0:1, :], ps[64:65, 2, :]).then_inc(s_epv.h, 1)
                s_epv.n += 1
                s_epv.w(v)
                v.reciprocal(rrow[0:1, :], srow[0:1, :]).then_inc(s_epv.h, 0) if False else None
                v.reciprocal(rrow[0:1, :], srow[0:1, :]).then_inc(s_bc.h, 1)
                s_bc.n += 1

            @block.tensor
            def _(t_, gh=gh):
                s_bc.w(t_)
                if gh > 0:
                    t_.wait_ge(s_epv.h, 3 * gh - 1)
                t_.matmul(ps[0:64, 4, :], oner[0:1, 0:64], rrow[0:1, :], start=True, stop=True).then_inc(s_bc.h, 1)
                s_bc.n += 1

            @block.vector
            def _(v, c=c, h=h, gh=gh):
                s_bc.w(v)
                v.tensor_copy(rbs[0:64, :], ps[0:64, 4, :]).then_inc(s_epv.h, 1)
                s_epv.n += 1
                s_epv.w(v)
                v.tensor_mul(a2[0:64, h, :], ps[0:64, 2, :], rbs[0:64, :]).then_inc(s_epv.h, 1)
                s_epv.n += 1

        EPC = s_epv.n

        @block.tensor
        def _(t_, c=c):
            s_epv.w(t_, EPC)
            for pp2 in range(KD):
                if pp2 >= 1:
                    s_ppk.w(t_, 16 * c + 2 * pp2)
                t_.matmul(ps[:, 7, :], id64a[0:64, :], a2[0:64, 2 * pp2, :], start=True, stop=False)
                mm = t_.matmul(ps[:, 7, :], id64b[0:64, :], a2[0:64, 2 * pp2 + 1, :], start=False, stop=True)
                s_ppk.inc(mm)

        @block.vector
        def _(v, c=c):
            for pp2 in range(KD):
                s_ppk.w(v, 16 * c + 2 * pp2 + 1)
                v.tensor_copy(a2p[:, pp2, :], ps[:, 7, :]).then_inc(s_ppk.h, 1)
                s_ppk.n += 1

        @block.sync
        def _(sy, c=c):
            if c == 0:
                sy.wait_ge(s_vst.h, VSTF)
            else:
                sy.wait_ge(s_out.h, 16 * c)
            sy.dma_start(xr[:, :, :], x3[:, :, c * CH:(c + 1) * CH]).then_inc(s_xres.h, 16)
            s_xres.n += 16

        @block.tensor
        def _(t_, c=c):
            s_ppk.w(t_)
            for o in range(KD):
                gidx = c * KD + o
                if gidx >= 2:
                    s_res.w(t_, gidx - 1)
                bank = 5 + gidx % 2
                mm = None
                for t in range(KD):
                    mm = t_.matmul(ps[:, bank, :], wo[:, t, o * 128:(o + 1) * 128],
                                   a2p[:, t, :], start=(t == 0), stop=(t == KD - 1))
                s_omm.inc(mm)

        @block.vector
        def _(v, c=c):
            s_xres.w(v)
            if c == 0:
                s_cst.w(v, 16)
            for o in range(KD):
                gidx = c * KD + o
                s_omm.w(v, gidx + 1)
                bank = 5 + gidx % 2
                v.scalar_tensor_tensor(xr[:, o, :], ps[:, bank, :], bo[:, o:o + 1],
                                       xr[:, o, :], ALU.add, ALU.add).then_inc(s_res.h, 1)
                s_res.n += 1

        @block.sync
        def _(sy, c=c):
            sy.wait_ge(s_res.h, s_res.n)
            sy.dma_start(xo3[:, :, c * CH:(c + 1) * CH], xr[:, :, :]).then_inc(s_out.h, 16)
            s_out.n += 16

    @block.sync
    def _(sy):
        sy.wait_ge(s_out.h, 16 * NCH)


def build_l1attn_program(T):
    nc = bass.Bass("TRN2", target_bir_lowering=False)
    io = {
        "xT_in": nc.dram_tensor("xT_in", [128, KD * T], F32, kind="ExternalInput"),
        "xT_other": nc.dram_tensor("xT_other", [128, KD * T], F32, kind="ExternalInput"),
        "xT_out": nc.dram_tensor("xT_out", [128, KD * T], F32, kind="ExternalOutput"),
        "Wq": nc.dram_tensor("Wq", [D, D], F32, kind="ExternalInput"),
        "Wk": nc.dram_tensor("Wk", [D, D], F32, kind="ExternalInput"),
        "Wv": nc.dram_tensor("Wv", [D, D], F32, kind="ExternalInput"),
        "wor": nc.dram_tensor("wor", [D, D], F32, kind="ExternalInput"),
        "bor": nc.dram_tensor("bor", [128, KD], F32, kind="ExternalInput"),
        "identf": nc.dram_tensor("identf", [128, 128], F32, kind="ExternalInput"),
        "id64a": nc.dram_tensor("id64a", [128, 128], F32, kind="ExternalInput"),
        "id64b": nc.dram_tensor("id64b", [128, 128], F32, kind="ExternalInput"),
        "cck_out": nc.dram_tensor("cck_out", [256, KD * T], BF16),
        "ccv_out": nc.dram_tensor("ccv_out", [2 * T, 16 * 65], BF16),
    }
    with contextlib.ExitStack() as ctx:
        block = ctx.enter_context(nc.Block())
        emit_l1attn_sublayer(nc, ctx, block, T, io)
    return nc


def l1attn_host_inputs(T, core_id, x_slice, Wq, Wk, Wv, Wo, bo_, x_other=None):
    xT = np.ascontiguousarray(x_slice.reshape(T, KD, 128).transpose(2, 1, 0).reshape(128, KD * T))
    xTo = xT if x_other is None else np.ascontiguousarray(
        x_other.reshape(T, KD, 128).transpose(2, 1, 0).reshape(128, KD * T))
    id64a = np.zeros((128, 128), dtype=np.float32)
    id64b = np.zeros((128, 128), dtype=np.float32)
    id64a[np.arange(64), np.arange(64)] = 1.0
    id64b[np.arange(64), np.arange(64) + 64] = 1.0
    return {
        "xT_in": xT,
        "xT_other": xTo,
        "Wq": np.ascontiguousarray(Wq), "Wk": np.ascontiguousarray(Wk),
        "Wv": np.ascontiguousarray(Wv), "wor": np.ascontiguousarray(Wo),
        "bor": np.ascontiguousarray(bo_.reshape(KD, 128).T),
        "identf": np.eye(128, dtype=np.float32),
        "id64a": id64a, "id64b": id64b,
    }



II = 2816
NIT = II // 128
ICW = [512, 512, 512, 512, 512, 256]
NIC = len(ICW)
NIC_NT = [4, 4, 4, 4, 4, 2]


def emit_l1ffn_sublayer(nc, ctx, block, T, io, tag="f1"):
    NCH = T // CH
    NTT = T // 128

    sb = lambda name, shape, dt: ctx.enter_context(nc.sbuf_tensor(name + tag, shape, dt))
    xqT = sb("xqT", [128, KD, T], BF16)
    oacc = sb("oacc", [128, NTT, D], F32)
    absx = sb("absx", [128, II], F32)
    sgc = sb("sgc", [128, II], BF16)
    qa = sb("qa", [128, II], F32)
    bmq = sb("bmq", [128, II // 16], F32)
    bmr = sb("bmr", [128, II // 16], F32)
    idf = sb("idf", [128, 128], F32)
    idb = sb("idb", [128, 128], BF16)
    onec = sb("onec", [128, 1], F32)
    oner = sb("oner", [1, 128], F32)
    dmy = sb("dmy", [128, 512], F32)

    ps = ctx.enter_context(nc.psum_tensor("ps" + tag, [128, 8, 512], F32))

    S = lambda name: Sem(ctx, nc, name + tag)
    s_cst, s_idb, s_ones = S("s_cst"), S("s_idb"), S("s_ones")
    s_x, s_xsqv, s_sqmm = S("s_x"), S("s_xsqv"), S("s_sqmm")
    s_vst, s_sst, s_rbab = S("s_vst"), S("s_sst"), S("s_rbab")
    s_qabs, s_qi, s_qv, s_qtp = S("s_qabs"), S("s_qi"), S("s_qv"), S("s_qtp")
    s_tpa, s_aev, s_atp = S("s_tpa"), S("s_aev"), S("s_atp")
    s_tpb, s_aev2 = S("s_tpb"), S("s_aev2")
    s_wq = [S("s_wqa"), S("s_wqb")]
    s_wst = [S("s_wsta"), S("s_wstb")]
    s_wev = S("s_wev")
    s_wld = [S("s_wlda"), S("s_wldb")]
    s_gmm, s_umm, s_hs, s_hv = S("s_gmm"), S("s_umm"), S("s_hs"), S("s_hv")
    s_hev, s_dmm, s_ov = S("s_hev"), S("s_dmm"), S("s_ov")
    s_x2, s_ctp, s_cev, s_out = S("s_x2"), S("s_ctp"), S("s_cev"), S("s_out")

    x3 = io["xT_in"].rearrange("p (t j) -> p t j", t=KD)
    xo3 = io["xT_out"].rearrange("p (t j) -> p t j", t=KD)
    wg2 = io["w_gate"].rearrange("(it p) d -> it p d", p=128)
    wu2 = io["w_up"].rearrange("(it p) d -> it p d", p=128)
    wd2 = io["w_down"].rearrange("(t p) i -> t p i", p=128)
    gq3 = io["gq_dram"].rearrange("p (t i) -> p t i", t=KD)
    uq3 = io["uq_dram"].rearrange("p (t i) -> p t i", t=KD)
    dq3 = io["dq_dram"].rearrange("p (it o) -> p it o", it=NIT)

    qs = {"nq": 0, "qi": 0, "tpb": 0}

    def col1(buf, off, n):
        return buf[:, off:off + n].rearrange("p (n k) -> p n k", k=1)

    def pair(buf, off, n):
        return buf[:, off:off + 2 * n].rearrange("p (n k) -> p n k", k=2)

    def bc16(buf, nb):
        a = buf[:, :]
        return bass.AP(a.tensor, a.offset, [[II // 16, 128], [1, nb], [0, 16]])

    def v16(buf, w):
        return buf[:, 0:w].rearrange("p (n k) -> p n k", k=16)

    def qd_scalar_pre(s, k, src, W):
        s_qv.w(s, k)
        s.activation(absx[:, 0:W], src, AF.Abs).then_inc(s_qabs.h, 1)
        s_qabs.n += 1
        s_qtp.w(s, k)
        s.activation(sgc[:, 0:W], src, AF.Sign).then_inc(s_qabs.h, 1)
        s_qabs.n += 1

    def qd_vector(v, k, W):
        NBW = W // 16

        def op(instr):
            instr.then_inc(s_qi.h, 1)
            s_qi.n += 1
            qs["qi"] += 1

        def wi():
            v.wait_ge(s_qi.h, qs["qi"])

        v.wait_ge(s_qabs.h, 2 * k + 1)
        s_qtp.w(v, k)
        op(v.tensor_tensor(col1(qa, 0, W // 2), pair(absx, 0, W // 2)[:, :, 0:1],
                           pair(absx, 0, W // 2)[:, :, 1:2], ALU.max))
        wi()
        op(v.tensor_tensor(col1(qa, W // 2, W // 4), pair(qa, 0, W // 4)[:, :, 0:1],
                           pair(qa, 0, W // 4)[:, :, 1:2], ALU.max))
        wi()
        op(v.tensor_tensor(col1(qa, 3 * W // 4, W // 8), pair(qa, W // 2, W // 8)[:, :, 0:1],
                           pair(qa, W // 2, W // 8)[:, :, 1:2], ALU.max))
        wi()
        op(v.tensor_tensor(col1(bmq, 0, NBW), pair(qa, 3 * W // 4, NBW)[:, :, 0:1],
                           pair(qa, 3 * W // 4, NBW)[:, :, 1:2], ALU.max))
        wi()
        op(v.tensor_scalar(bmq[:, 0:NBW], bmq[:, 0:NBW], 6e-12, None, ALU.max))
        wi()
        op(v.reciprocal(bmr[:, 0:NBW], bmq[:, 0:NBW]))
        wi()
        op(v.tensor_scalar(bmr[:, 0:NBW], bmr[:, 0:NBW], 6.0, None, ALU.mult))
        wi()
        op(v.tensor_tensor(v16(absx, W), v16(absx, W), bc16(bmr, NBW), ALU.mult))
        wi()
        op(v.tensor_scalar(qa[:, 0:W], absx[:, 0:W], 0.25, None, ALU.is_gt))
        for thr in (0.75, 1.25, 1.75):
            wi()
            op(v.scalar_tensor_tensor(qa[:, 0:W], absx[:, 0:W], thr, qa[:, 0:W],
                                      ALU.is_gt, ALU.add))
        wi()
        op(v.tensor_scalar(qa[:, 0:W], qa[:, 0:W], 0.5, None, ALU.mult))
        for thr in (2.5, 3.5, 5.0, 5.0):
            wi()
            op(v.scalar_tensor_tensor(qa[:, 0:W], absx[:, 0:W], thr, qa[:, 0:W],
                                      ALU.is_gt, ALU.add))
        wi()
        op(v.tensor_scalar(bmq[:, 0:NBW], bmq[:, 0:NBW], 1.0 / 6.0, None, ALU.mult))
        v.wait_ge(s_qabs.h, 2 * k + 2)
        wi()
        op(v.tensor_tensor(qa[:, 0:W], qa[:, 0:W], sgc[:, 0:W], ALU.mult))
        wi()
        v.tensor_tensor(v16(qa, W), v16(qa, W), bc16(bmq, NBW), ALU.mult).then_inc(s_qv.h, 1)
        s_qv.n += 1
        qs["nq"] = k + 1

    @block.sync
    def _(sy):
        sy.dma_start(idf[:, :], io["identf"][:, :]).then_inc(s_cst.h, 16)
        s_cst.n += 16

    @block.gpsimd
    def _(g):
        g.dma_start(idb[:, :], io["identf"][:, :]).then_inc(s_idb.h, 16)
        s_idb.n += 16
        g.memset(onec[:, :], 1.0).then_inc(s_ones.h, 1)
        g.memset(oner[:, :], 1.0).then_inc(s_ones.h, 1)
        g.memset(dmy[:, :], 1.0).then_inc(s_ones.h, 1)
        s_ones.n += 3

    with contextlib.ExitStack() as actx:
        asb = lambda name, shape, dt: actx.enter_context(nc.sbuf_tensor(name + tag, shape, dt))
        xr = asb("xr", [128, KD, CH], F32)
        xnt = asb("xnt", [128, KD, CH], F32)
        xtk = asb("xtk", [128, D], F32)
        xsq = asb("xsq", [128, 1, CH], F32)
        mrow = asb("mrow", [1, CH], F32)
        msrow = asb("msrow", [1, CH], F32)
        lnv = msrow
        vrow = asb("vrow", [1, CH], F32)
        rstd = asb("rstd", [1, CH], F32)
        acoef = asb("acoef", [1, CH], F32)

        for c in range(NCH):
            csl = slice(c * CH, (c + 1) * CH)

            @block.sync
            def _(sy, c=c):
                if c > 0:
                    sy.wait_ge(s_vst.h, 22 * c)
                sy.dma_start(xr[:, :, :], x3[:, :, slice(c * CH, (c + 1) * CH)]).then_inc(s_x.h, 16)
                s_x.n += 16

            @block.vector
            def _(v, c=c):
                s_x.w(v)
                for t in range(KD):
                    m = c * KD + t
                    if m >= 1:
                        s_sqmm.w(v, m)
                    v.tensor_mul(xsq[:, 0, :], xr[:, t, :], xr[:, t, :]).then_inc(s_xsqv.h, 1)
                    s_xsqv.n += 1

            @block.tensor
            def _(t_, c=c):
                s_ones.w(t_, 2)
                s_x.w(t_)
                if c > 0:
                    s_vst.w(t_, 22 * (c - 1) + 2)
                for t in range(KD):
                    t_.matmul(ps[0:1, 6, :], onec[:, :], xr[:, t, :],
                              start=(t == 0), stop=(t == KD - 1))
                for t in range(KD):
                    s_xsqv.w(t_, c * KD + t + 1)
                    mm = t_.matmul(ps[0:1, 5, :], onec[:, :], xsq[:, 0, :],
                                   start=(t == 0), stop=(t == KD - 1))
                    s_sqmm.inc(mm)

            @block.vector
            def _(v, c=c):
                s_sqmm.w(v)
                v.tensor_scalar(mrow[:, :], ps[0:1, 6, :], 1.0 / D, None, ALU.mult).then_inc(s_vst.h, 1)
                v.tensor_scalar(msrow[:, :], ps[0:1, 5, :], 1.0 / D, None, ALU.mult).then_inc(s_vst.h, 1)
                s_vst.n += 2
                s_vst.w(v)
                v.scalar_tensor_tensor(vrow[:, :], mrow[:, :], -1.0, mrow[:, :], ALU.mult, ALU.mult).then_inc(s_vst.h, 1)
                s_vst.n += 1
                s_vst.w(v)
                v.tensor_add(vrow[:, :], vrow[:, :], msrow[:, :]).then_inc(s_vst.h, 1)
                s_vst.n += 1
                s_vst.w(v)
                v.tensor_scalar(vrow[:, :], vrow[:, :], EPS, None, ALU.add).then_inc(s_vst.h, 1)
                s_vst.n += 1

            @block.scalar
            def _(s, c=c):
                s_vst.w(s)
                s.activation(lnv[:, :], vrow[:, :], AF.Ln).then_inc(s_sst.h, 1)
                s_sst.n += 1
                s_sst.w(s)
                s.activation(rstd[:, :], lnv[:, :], AF.Exp, scale=-0.5).then_inc(s_sst.h, 1)
                s_sst.n += 1

            @block.vector
            def _(v, c=c):
                s_sst.w(v)
                v.scalar_tensor_tensor(acoef[:, :], mrow[:, :], -1.0, rstd[:, :], ALU.mult, ALU.mult).then_inc(s_vst.h, 1)
                s_vst.n += 1

            @block.tensor
            def _(t_, c=c):
                s_vst.w(t_)
                t_.matmul(ps[:, 3, :], oner[0:1, :], rstd[:, :], start=True, stop=True).then_inc(s_rbab.h, 1)
                t_.matmul(ps[:, 4, :], oner[0:1, :], acoef[:, :], start=True, stop=True).then_inc(s_rbab.h, 1)
                s_rbab.n += 2

            @block.vector
            def _(v, c=c):
                s_rbab.w(v)
                if c > 0:
                    s_atp.w(v, 4 * c)
                for t in range(KD):
                    s_vst.w(v)
                    v.tensor_mul(xnt[:, t, :], xr[:, t, :], ps[:, 3, :]).then_inc(s_vst.h, 1)
                    s_vst.n += 1
                    s_vst.w(v)
                    v.tensor_add(xnt[:, t, :], xnt[:, t, :], ps[:, 4, :]).then_inc(s_vst.h, 1)
                    s_vst.n += 1

            for lt in range(4):
                k = qs["nq"]
                tt = 4 * c + lt
                tsl = slice(lt * 128, (lt + 1) * 128)

                @block.tensor
                def _(t_, c=c, lt=lt, tt=tt, tsl=tsl, k=k):
                    s_cst.w(t_, 16)
                    s_vst.w(t_, 22 * (c + 1))
                    s_aev.w(t_, 8 * tt)
                    for t in range(KD):
                        mm = t_.transpose(ps[:, t % 2, (t // 2) * 128:(t // 2) * 128 + 128],
                                          xnt[:, t, tsl], idf[:, :])
                        if t < KD - 1:
                            s_tpa.inc(mm)
                        else:
                            s_atp.inc(mm)

                @block.scalar
                def _(s, c=c, lt=lt, tt=tt, k=k):
                    base = 7 * tt
                    for t in range(KD):
                        if t < KD - 1:
                            s_tpa.w(s, base + t + 1)
                        else:
                            s_atp.w(s, tt + 1)
                        s.activation(xtk[:, t * 128:(t + 1) * 128],
                                     ps[:, t % 2, (t // 2) * 128:(t // 2) * 128 + 128],
                                     AF.Copy).then_inc(s_aev.h, 1)
                        s_aev.n += 1
                    s_aev.w(s)
                    qd_scalar_pre(s, k, xtk[:, :], D)

                @block.vector
                def _(v, k=k):
                    qd_vector(v, k, D)

                @block.tensor
                def _(t_, c=c, lt=lt, tt=tt, k=k):
                    s_qv.w(t_, k + 1)
                    s_aev2.w(t_, 8 * tt)
                    for t in range(KD):
                        mm = t_.transpose(ps[:, 2 + t % 2, (t // 2) * 128:(t // 2) * 128 + 128],
                                          qa[:, t * 128:(t + 1) * 128], idf[:, :])
                        if t < KD - 1:
                            s_tpb.inc(mm)
                            qs["tpb"] += 1
                        else:
                            s_qtp.inc(mm)

                @block.scalar
                def _(s, c=c, lt=lt, tt=tt, k=k):
                    base = qs["tpb"] - 7
                    for t in range(KD):
                        if t < KD - 1:
                            s_tpb.w(s, base + t + 1)
                        else:
                            s_qtp.w(s, k + 1)
                        s.activation(xqT[:, t, c * CH + lt * 128:c * CH + lt * 128 + 128],
                                     ps[:, 2 + t % 2, (t // 2) * 128:(t // 2) * 128 + 128],
                                     AF.Copy).then_inc(s_aev2.h, 1)
                        s_aev2.n += 1

    NQ_A = qs["nq"]
    VST_A = s_vst.n
    ATP_A = s_atp.n
    RBAB_A = s_rbab.n
    QABS_A = s_qabs.n

    with contextlib.ExitStack() as wctx:
        wsb = lambda name, shape, dt: wctx.enter_context(nc.sbuf_tensor(name + tag, shape, dt))
        wrow = wsb("wrow", [128, 2, II], F32)
        stg = wsb("stg", [128, 2, NIT, 128], BF16)

        jobs = ([("g", it, D, 8) for it in range(NIT)]
                + [("u", it, D, 8) for it in range(NIT)]
                + [("d", t, II, NIT) for t in range(KD)])
        wtp = [0]

        for ji, (kind, it, W, ntp) in enumerate(jobs):
            k = qs["nq"]

            @block.sync
            def _(sy, ji=ji, kind=kind, it=it, W=W):
                if ji == 0:
                    sy.wait_ge(s_vst.h, VST_A)
                    sy.wait_ge(s_atp.h, ATP_A)
                    sy.wait_ge(s_qabs.h, QABS_A)
                    sy.wait_ge(s_rbab.h, RBAB_A)
                    sy.wait_ge(s_sqmm.h, s_sqmm.n)
                    sy.wait_ge(s_sst.h, s_sst.n)
                if ji >= 2:
                    sy.wait_ge(s_qabs.h, 2 * (NQ_A + ji - 1))
                src = wg2[it] if kind == "g" else (wu2[it] if kind == "u" else wd2[it])
                sy.dma_start(wrow[:, ji % 2, 0:W], src).then_inc(s_wq[ji % 2].h, 16)
                s_wq[ji % 2].n += 16

            @block.scalar
            def _(s, ji=ji, k=k, W=W):
                s.wait_ge(s_wq[ji % 2].h, 16 * (ji // 2 + 1))
                qd_scalar_pre(s, k, wrow[:, ji % 2, 0:W], W)

            @block.vector
            def _(v, k=k, W=W):
                qd_vector(v, k, W)

            @block.tensor
            def _(t_, ji=ji, k=k, ntp=ntp):
                s_qv.w(t_, k + 1)
                if ji == 0:
                    s_ones.w(t_, 3)
                    s_aev.w(t_, s_aev.n)
                    s_aev2.w(t_, s_aev2.n)
                for j in range(ntp):
                    m = wtp[0] + j
                    if m >= 8:
                        s_wev.w(t_, m - 8 + 1)
                    mm = t_.transpose(ps[:, m % 2, ((m // 2) % 4) * 128:((m // 2) % 4) * 128 + 128],
                                      qa[:, j * 128:(j + 1) * 128], idf[:, :])
                    if j < ntp - 1:
                        s_tpb.inc(mm)
                        qs["tpb"] += 1
                    else:
                        s_qtp.inc(mm)
                    t_.matmul(ps[:, 6, 0:512], idf[:, :], dmy[:, :],
                              start=True, stop=True)

            @block.scalar
            def _(s, ji=ji, k=k, ntp=ntp):
                base = qs["tpb"] - (ntp - 1)
                if ji >= 2:
                    s.wait_ge(s_wst[ji % 2].h, 16 * (ji // 2))
                for j in range(ntp):
                    m = wtp[0] + j
                    slot = m % 8
                    if j < ntp - 1:
                        s_tpb.w(s, base + j + 1)
                    else:
                        s_qtp.w(s, k + 1)
                    s.activation(stg[:, ji % 2, j, :],
                                 ps[:, m % 2, ((m // 2) % 4) * 128:((m // 2) % 4) * 128 + 128],
                                 AF.Copy).then_inc(s_wev.h, 1)
                    s_wev.n += 1

            @block.gpsimd
            def _(g, ji=ji, kind=kind, it=it, ntp=ntp):
                g.wait_ge(s_wev.h, s_wev.n)
                if kind == "g":
                    dst = gq3[:, :, it * 128:(it + 1) * 128]
                elif kind == "u":
                    dst = uq3[:, :, it * 128:(it + 1) * 128]
                else:
                    dst = dq3[:, :, it * 128:(it + 1) * 128]
                g.dma_start(dst, stg[:, ji % 2, 0:ntp, :]).then_inc(s_wst[ji % 2].h, 16)
                s_wst[ji % 2].n += 16

            wtp[0] += ntp

    NQ_W = qs["nq"]
    WST_A_, WST_B_ = s_wst[0].n, s_wst[1].n

    with contextlib.ExitStack() as bctx:
        bsb = lambda name, shape, dt: bctx.enter_context(nc.sbuf_tensor(name + tag, shape, dt))
        wg_r = bsb("wg_r", [128, KD, 2, 512], BF16)
        wu_r = bsb("wu_r", [128, KD, 2, 512], BF16)
        wd_r = bsb("wd_r", [128, 4, 2, D], BF16)
        ht = bsb("ht", [128, 512], F32)
        hqT = bsb("hqT", [128, 2, 4, 128], BF16)

        for ic in range(NIC):
            icw = ICW[ic]
            nit = NIC_NT[ic]
            icl = slice(ic * 512, ic * 512 + icw)

            @block.sync
            def _(sy, ic=ic, icw=icw, nit=nit, icl=icl):
                sy.wait_ge(s_wst[0].h, WST_A_)
                sy.wait_ge(s_wst[1].h, WST_B_)
                if ic == 0:
                    sy.wait_ge(s_qabs.h, 2 * NQ_W)
                if ic >= 2:
                    sy.wait_ge(s_gmm.h, NTT * (ic - 1))
                    sy.wait_ge(s_umm.h, NTT * (ic - 1))
                    sy.wait_ge(s_dmm.h, 2 * NTT * (ic - 1))
                sy.dma_start(wg_r[:, :, ic % 2, 0:icw], gq3[:, :, icl]).then_inc(s_wld[ic % 2].h, 16)
                sy.dma_start(wu_r[:, :, ic % 2, 0:icw], uq3[:, :, icl]).then_inc(s_wld[ic % 2].h, 16)
                sy.dma_start(wd_r[:, 0:nit, ic % 2, :], dq3[:, ic * 4:ic * 4 + nit, :]).then_inc(s_wld[ic % 2].h, 16)
                s_wld[ic % 2].n += 48

            for tt in range(NTT):
                g_it = ic * NTT + tt
                k = qs["nq"]
                ttsl = slice(tt * 128, (tt + 1) * 128)

                @block.tensor
                def _(t_, ic=ic, tt=tt, g_it=g_it, icw=icw, ttsl=ttsl):
                    s_wld[ic % 2].w(t_)
                    if g_it == 0:
                        s_aev.w(t_, s_aev.n)
                        s_aev2.w(t_, s_aev2.n)
                        s_wev.w(t_, s_wev.n)
                        s_vst.w(t_, VST_A)
                    s_hs.w(t_, g_it)
                    s_hv.w(t_, g_it)
                    for t in range(KD):
                        mm = t_.matmul(ps[:, 0, 0:icw], xqT[:, t, ttsl],
                                       wg_r[:, t, ic % 2, 0:icw],
                                       start=(t == 0), stop=(t == KD - 1))
                    s_gmm.inc(mm)
                    for t in range(KD):
                        mm = t_.matmul(ps[:, 1, 0:icw], xqT[:, t, ttsl],
                                       wu_r[:, t, ic % 2, 0:icw],
                                       start=(t == 0), stop=(t == KD - 1))
                    s_umm.inc(mm)

                @block.scalar
                def _(s, g_it=g_it, icw=icw):
                    s_gmm.w(s, g_it + 1)
                    s.activation(ht[:, 0:icw], ps[:, 0, 0:icw], AF.Silu).then_inc(s_hs.h, 1)
                    s_hs.n += 1

                @block.vector
                def _(v, g_it=g_it, icw=icw):
                    s_hs.w(v, g_it + 1)
                    s_umm.w(v, g_it + 1)
                    v.tensor_mul(ht[:, 0:icw], ht[:, 0:icw], ps[:, 1, 0:icw]).then_inc(s_hv.h, 1)
                    s_hv.n += 1

                @block.scalar
                def _(s, g_it=g_it, k=k, icw=icw):
                    s_hv.w(s, g_it + 1)
                    qd_scalar_pre(s, k, ht[:, 0:icw], icw)

                @block.vector
                def _(v, k=k, icw=icw):
                    qd_vector(v, k, icw)

                @block.tensor
                def _(t_, g_it=g_it, k=k, nit=nit):
                    s_qv.w(t_, k + 1)
                    s_hev.w(t_, s_hev.n)
                    for j in range(nit):
                        mm = t_.transpose(ps[:, 4 + j % 2, (j // 2) * 128:(j // 2) * 128 + 128],
                                          qa[:, j * 128:(j + 1) * 128], idf[:, :])
                        if j < nit - 1:
                            s_tpb.inc(mm)
                            qs["tpb"] += 1
                        else:
                            s_qtp.inc(mm)

                @block.scalar
                def _(s, tt=tt, g_it=g_it, k=k, nit=nit):
                    base = qs["tpb"] - (nit - 1)
                    if g_it >= 2:
                        s_dmm.w(s, 2 * (g_it - 1))
                    for j in range(nit):
                        if j < nit - 1:
                            s_tpb.w(s, base + j + 1)
                        else:
                            s_qtp.w(s, k + 1)
                        s.activation(hqT[:, tt % 2, j, :],
                                     ps[:, 4 + j % 2, (j // 2) * 128:(j // 2) * 128 + 128],
                                     AF.Copy).then_inc(s_hev.h, 1)
                        s_hev.n += 1

                @block.tensor
                def _(t_, ic=ic, tt=tt, g_it=g_it, nit=nit):
                    s_hev.w(t_, s_hev.n)
                    s_ov.w(t_, 2 * g_it)
                    for j in range(nit):
                        mm = t_.matmul(ps[:, 2, :], hqT[:, tt % 2, j, :],
                                       wd_r[:, j, ic % 2, 0:512],
                                       start=(j == 0), stop=(j == nit - 1))
                    s_dmm.inc(mm)
                    for j in range(nit):
                        mm = t_.matmul(ps[:, 3, :], hqT[:, tt % 2, j, :],
                                       wd_r[:, j, ic % 2, 512:1024],
                                       start=(j == 0), stop=(j == nit - 1))
                    s_dmm.inc(mm)

                @block.vector
                def _(v, ic=ic, tt=tt, g_it=g_it):
                    s_dmm.w(v, 2 * (g_it + 1))
                    if ic == 0:
                        v.tensor_scalar(oacc[:, tt, 0:512], ps[:, 2, :], 0.0, None, ALU.add).then_inc(s_ov.h, 1)
                        s_ov.n += 1
                        v.tensor_scalar(oacc[:, tt, 512:1024], ps[:, 3, :], 0.0, None, ALU.add).then_inc(s_ov.h, 1)
                        s_ov.n += 1
                    else:
                        v.tensor_add(oacc[:, tt, 0:512], oacc[:, tt, 0:512], ps[:, 2, :]).then_inc(s_ov.h, 1)
                        s_ov.n += 1
                        v.tensor_add(oacc[:, tt, 512:1024], oacc[:, tt, 512:1024], ps[:, 3, :]).then_inc(s_ov.h, 1)
                        s_ov.n += 1

    GMM_ALL, UMM_ALL, DMM_ALL = s_gmm.n, s_umm.n, s_dmm.n
    QABS_ALL = s_qabs.n

    with contextlib.ExitStack() as cctx:
        csb = lambda name, shape, dt: cctx.enter_context(nc.sbuf_tensor(name + tag, shape, dt))
        xr2 = csb("xr2", [128, KD, CH], F32)
        xout = csb("xout", [128, KD, CH], F32)

        for c in range(NCH):
            csl = slice(c * CH, (c + 1) * CH)

            @block.sync
            def _(sy, c=c):
                if c == 0:
                    sy.wait_ge(s_gmm.h, GMM_ALL)
                    sy.wait_ge(s_umm.h, UMM_ALL)
                    sy.wait_ge(s_dmm.h, DMM_ALL)
                    sy.wait_ge(s_qabs.h, QABS_ALL)
                if c > 0:
                    sy.wait_ge(s_cev.h, 32 * c)
                sy.dma_start(xr2[:, :, :], x3[:, :, slice(c * CH, (c + 1) * CH)]).then_inc(s_x2.h, 16)
                s_x2.n += 16

            for lt in range(4):
                tt = 4 * c + lt
                tsl = slice(lt * 128, (lt + 1) * 128)

                @block.tensor
                def _(t_, c=c, lt=lt, tt=tt):
                    s_ov.w(t_, 2 * (tt + 1) + 2 * NTT * (NIC - 1))
                    s_cev.w(t_, 8 * tt)
                    for t in range(KD):
                        mm = t_.transpose(ps[:, 6 + t % 2, (t // 2) * 128:(t // 2) * 128 + 128],
                                          oacc[:, tt, t * 128:(t + 1) * 128], idf[:, :])
                        s_ctp.inc(mm)

                @block.vector
                def _(v, c=c, lt=lt, tt=tt, tsl=tsl):
                    s_ctp.w(v)
                    s_x2.w(v, 16 * (c + 1))
                    if lt == 0 and c > 0:
                        s_out.w(v, 16 * c)
                    for t in range(KD):
                        v.scalar_tensor_tensor(xout[:, t, tsl],
                                               ps[:, 6 + t % 2, (t // 2) * 128:(t // 2) * 128 + 128],
                                               0.5, xr2[:, t, tsl],
                                               ALU.mult, ALU.add).then_inc(s_cev.h, 1)
                        s_cev.n += 1

            @block.gpsimd
            def _(g, c=c):
                g.wait_ge(s_cev.h, 32 * (c + 1))
                g.dma_start(xo3[:, :, slice(c * CH, (c + 1) * CH)], xout[:, :, :]).then_inc(s_out.h, 16)
                s_out.n += 16

    @block.sync
    def _(sy):
        sy.wait_ge(s_out.h, 16 * NCH)


def build_l1ffn_program(T):
    nc = bass.Bass("TRN2", target_bir_lowering=False)
    io = {
        "xT_in": nc.dram_tensor("xT_in", [128, KD * T], F32, kind="ExternalInput"),
        "xT_out": nc.dram_tensor("xT_out", [128, KD * T], F32, kind="ExternalOutput"),
        "w_gate": nc.dram_tensor("w_gate", [II, D], F32, kind="ExternalInput"),
        "w_up": nc.dram_tensor("w_up", [II, D], F32, kind="ExternalInput"),
        "w_down": nc.dram_tensor("w_down", [D, II], F32, kind="ExternalInput"),
        "identf": nc.dram_tensor("identf", [128, 128], F32, kind="ExternalInput"),
        "gq_dram": nc.dram_tensor("gq_dram", [128, KD * II], BF16),
        "uq_dram": nc.dram_tensor("uq_dram", [128, KD * II], BF16),
        "dq_dram": nc.dram_tensor("dq_dram", [128, NIT * D], BF16),
    }
    with contextlib.ExitStack() as ctx:
        block = ctx.enter_context(nc.Block())
        emit_l1ffn_sublayer(nc, ctx, block, T, io)
    return nc


def l1ffn_host_inputs(T, x_slice, w_gate, w_up, w_down):
    xT = np.ascontiguousarray(x_slice.reshape(T, KD, 128).transpose(2, 1, 0).reshape(128, KD * T))
    return {
        "xT_in": xT,
        "w_gate": np.ascontiguousarray(w_gate),
        "w_up": np.ascontiguousarray(w_up),
        "w_down": np.ascontiguousarray(w_down),
        "identf": np.eye(128, dtype=np.float32),
    }


def build_moe_program(T):
    nc = bass.Bass("TRN2", target_bir_lowering=False)
    io = {
        "xT_in": nc.dram_tensor("xT_in", [128, KD * T], F32, kind="ExternalInput"),
        "xT_out": nc.dram_tensor("xT_out", [128, KD * T], F32, kind="ExternalOutput"),
        "rw": nc.dram_tensor("rw", [D, NE], F32, kind="ExternalInput"),
        "rb": nc.dram_tensor("rb", [NE, 1], F32, kind="ExternalInput"),
        "w1": nc.dram_tensor("w1", [NE, D, H_MOE], F32, kind="ExternalInput"),
        "b1r": nc.dram_tensor("b1r", [128, NE, NHT], F32, kind="ExternalInput"),
        "w2": nc.dram_tensor("w2", [NE, H_MOE, D], F32, kind="ExternalInput"),
        "b2r": nc.dram_tensor("b2r", [128, NE, KD], F32, kind="ExternalInput"),
        "identf": nc.dram_tensor("identf", [128, 128], F32, kind="ExternalInput"),
        "g_dram": nc.dram_tensor("g_dram", [NE, T], F32),
    }
    with contextlib.ExitStack() as ctx:
        block = ctx.enter_context(nc.Block())
        emit_moe_sublayer(nc, ctx, block, T, io)
    return nc


def moe_host_inputs(T, x_slice, rw, rb, w1, b1, w2, b2):
    xT = np.ascontiguousarray(x_slice.reshape(T, KD, 128).transpose(2, 1, 0).reshape(128, KD * T))
    return {
        "xT_in": xT,
        "rw": np.ascontiguousarray(rw),
        "rb": np.ascontiguousarray(rb.reshape(NE, 1)),
        "w1": np.ascontiguousarray(w1),
        "b1r": np.ascontiguousarray(b1.reshape(NE, NHT, 128).transpose(2, 0, 1)),
        "w2": np.ascontiguousarray(w2),
        "b2r": np.ascontiguousarray(b2.reshape(NE, KD, 128).transpose(2, 0, 1)),
        "identf": np.eye(128, dtype=np.float32),
    }


def xT_to_tokens(xT, T):
    return xT.reshape(128, KD, T).transpose(2, 1, 0).reshape(T, D)


def kernel(**inputs):
    T = 2048
    f32 = np.float32
    g = lambda k: np.ascontiguousarray(np.asarray(inputs[k], dtype=f32))
    x = g("L0_attn_x").reshape(8 * T, D)

    moe_nc = build_moe_program(T)
    l0a_nc = build_l0attn_program(T)
    l1f_nc = build_l1ffn_program(T)
    l1a_nc = build_l1attn_program(T)

    moe_maps = [moe_host_inputs(T, x[c * T:(c + 1) * T],
                                g("L0_ffn_router_w"), g("L0_ffn_router_b"),
                                g("L0_ffn_w1"), g("L0_ffn_b1"),
                                g("L0_ffn_w2"), g("L0_ffn_b2")) for c in range(8)]
    l0a_maps = [l0attn_host_inputs(T, c, x[c * T:(c + 1) * T],
                                   g("L0_attn_Wq"), g("L0_attn_Wk"), g("L0_attn_Wv"),
                                   g("L0_attn_Wfc"), g("L0_attn_gamma"),
                                   g("L0_attn_beta"), g("L0_attn_bfc")) for c in range(8)]
    l1f_maps = [l1ffn_host_inputs(T, x[c * T:(c + 1) * T],
                                  g("L1_ffn_w_gate"), g("L1_ffn_w_up"),
                                  g("L1_ffn_w_down")) for c in range(8)]
    l1a_maps = [l1attn_host_inputs(T, c, x[c * T:(c + 1) * T],
                                   g("L1_attn_Wq"), g("L1_attn_Wk"), g("L1_attn_Wv"),
                                   g("L1_attn_Wo"), g("L1_attn_bo")) for c in range(8)]

    def run(nc, maps, xTs, pair=False):
        for c in range(8):
            maps[c]["xT_in"] = xTs[c]
            if pair:
                maps[c]["xT_other"] = xTs[c ^ 1]
        res = run_bass_kernel_spmd(nc, maps, core_ids=list(range(8)))
        return [np.asarray(res.results[c]["xT_out"]) for c in range(8)]

    xTs = [moe_maps[c]["xT_in"] for c in range(8)]
    xTs = run(moe_nc, moe_maps, xTs)
    xTs = run(l0a_nc, l0a_maps, xTs, pair=True)
    xTs = run(moe_nc, moe_maps, xTs)
    xTs = run(l1f_nc, l1f_maps, xTs)
    xTs = run(l1a_nc, l1a_maps, xTs, pair=True)
    xTs = run(l1f_nc, l1f_maps, xTs)

    out = np.concatenate([xT_to_tokens(xTs[c], T) for c in range(8)], axis=0)
    return np.ascontiguousarray(out.reshape(4, 2 * T, D)).astype(np.float32)
